# version 18
# baseline (speedup 1.0000x reference)
import numpy as np
import ml_dtypes
import concourse.bass as bass
import concourse.mybir as mybir
from concourse.bass_utils import run_bass_kernel_spmd

F32 = mybir.dt.float32
BF16 = mybir.dt.bfloat16
I32 = mybir.dt.int32
AF = mybir.ActivationFunctionType
ALU = mybir.AluOpType
AX = mybir.AxisListType

D = 4096
L = 4096
NH = 16
HD = 256
NFB = 32
HALF = 512
SCALE = 1.0 / 16.0
LN_EPS = 1e-5
FFN_EPS = 1e-5
KB = 1024
PARANOID = False

ENGS = ("tensor", "scalar", "vector", "gpsimd", "sync")


def _build_tables():
    g = [1.0, 1.0, 1.0]
    n = 8
    masks = sorted(range(n), key=lambda m: (bin(m).count("1"), m))
    inv = {m: i for i, m in enumerate(masks)}
    C = np.zeros((n, n, n), dtype=np.float32)
    for i, a in enumerate(masks):
        for j, b in enumerate(masks):
            sw, at = 0, a >> 1
            while at:
                sw += bin(at & b).count("1")
                at >>= 1
            s = -1.0 if sw & 1 else 1.0
            for t in range(3):
                if a >> t & 1 and b >> t & 1:
                    s *= float(g[t])
            C[i, j, inv[a ^ b]] = s
    grades = np.array([bin(m).count("1") for m in masks])
    rev = np.where(grades * (grades - 1) // 2 % 2 == 1, -1.0, 1.0).astype(np.float32)
    met = np.array(
        [float(np.prod([g[t] for t in range(3) if m >> t & 1] + [1.0])) for m in masks],
        dtype=np.float32,
    )
    return C, rev, met


def _host_consts():
    C, rev, met = _build_tables()
    biv = C[:, :, [4, 5, 6]]
    d0 = np.zeros((128, 128), np.float32)
    for p in range(128):
        d0[p, p] = rev[p % 8] * met[p % 8] * SCALE
    G = np.zeros((3, 128, 128), np.float32)
    for c in range(3):
        blk = rev[:, None] * biv[:, :, c]
        for dd in range(16):
            G[c, dd * 8:(dd + 1) * 8, dd * 8:(dd + 1) * 8] = blk
    d0g = np.concatenate([d0, G[0], G[1], G[2]], axis=1)
    ident = np.eye(128, dtype=np.float32)
    ones = np.ones((128, 128), np.float32)
    return dict(
        d0g=d0g,
        ident_f32=ident,
        ident_bf16=ident.astype(ml_dtypes.bfloat16),
        ones_f32=ones,
        ones_bf16=ones.astype(ml_dtypes.bfloat16),
        invd=np.full((128, 1), 1.0 / 4096.0, ml_dtypes.bfloat16),
        epsc=np.full((1, 1), LN_EPS, np.float32),
        cm15=np.full((1, 128), -1.5, np.float32),
    )


def _swizzle_w(wT):
    return np.ascontiguousarray(wT.reshape(32, 128, 16, 256).transpose(2, 1, 0, 3))


def _swizzle_x(xT, perm):
    xs = xT.reshape(32, 128, 8, 512).transpose(2, 1, 0, 3)
    return np.ascontiguousarray(xs[perm])


class PSem:
    __slots__ = ("name", "n", "h")

    def __init__(self, name):
        self.name, self.n, self.h = name, 0, None


class Plan:
    def __init__(self):
        self.q = {e: [] for e in ENGS}
        self.wcache = {}

    def emit(self, e, fn):
        self.q[e].append(fn)

    def wait(self, e, sem, val):
        if val <= 0:
            return
        k = (e, sem.name)
        if self.wcache.get(k, -1) >= val:
            return
        self.wcache[k] = val
        self.emit(e, lambda eng, s=sem, v=val: eng.wait_ge(s.h, v))


def build_program(dbg=False, upto="full"):
    nc = bass.Bass("TRN2", target_bir_lowering=False, detect_race_conditions=False)

    x_sw = nc.dram_tensor("x_sw", [8, 128, NFB, 512], F32, kind="ExternalInput").ap()
    wq_sw = nc.dram_tensor("wq_sw", [16, 128, 32, 256], F32, kind="ExternalInput").ap()
    wk_sw = nc.dram_tensor("wk_sw", [16, 128, 32, 256], F32, kind="ExternalInput").ap()
    wv_sw = nc.dram_tensor("wv_sw", [16, 128, 32, 256], F32, kind="ExternalInput").ap()
    wf_sw = nc.dram_tensor("wf_sw", [16, 128, 32, 256], BF16, kind="ExternalInput").ap()
    bq_in = nc.dram_tensor("bq_r", [128, 32], F32, kind="ExternalInput").ap()
    bk_in = nc.dram_tensor("bk_r", [128, 32], F32, kind="ExternalInput").ap()
    bv_in = nc.dram_tensor("bv_r", [128, 32], F32, kind="ExternalInput").ap()
    bb_in = nc.dram_tensor("bias_r", [128, 32], F32, kind="ExternalInput").ap()
    bw_in = nc.dram_tensor("bw_flat", [1, 48], F32, kind="ExternalInput").ap()
    ws_in = nc.dram_tensor("wscale", [1, 1], F32, kind="ExternalInput").ap()
    d0g_in = nc.dram_tensor("d0g", [128, 512], F32, kind="ExternalInput").ap()
    idf_in = nc.dram_tensor("ident_f32", [128, 128], F32, kind="ExternalInput").ap()
    idb_in = nc.dram_tensor("ident_bf16", [128, 128], BF16, kind="ExternalInput").ap()
    onf_in = nc.dram_tensor("ones_f32", [128, 128], F32, kind="ExternalInput").ap()
    onb_in = nc.dram_tensor("ones_bf16", [128, 128], BF16, kind="ExternalInput").ap()
    invd_in = nc.dram_tensor("invd", [128, 1], BF16, kind="ExternalInput").ap()
    eps_in = nc.dram_tensor("epsc", [1, 1], F32, kind="ExternalInput").ap()
    cm15_in = nc.dram_tensor("cm15", [1, 128], F32, kind="ExternalInput").ap()

    outT = nc.dram_tensor("outT", [2, 128, NFB, 512], F32, kind="ExternalOutput").ap()

    xbd = nc.dram_tensor("xbd", [8, 128, NFB, 512], BF16, kind="Internal").ap()
    kd = nc.dram_tensor("kd", [32, 128, L], BF16, kind="Internal").ap()
    vd = nc.dram_tensor("vd", [NH, 8, 128, 4, HD], BF16, kind="Internal").ap()

    dbg_outs = {}
    if dbg:
        def dbgt(name, shape, dt):
            dbg_outs[name] = nc.dram_tensor(name, shape, dt, kind="ExternalOutput").ap()
        dbgt("dbg_xbd", [8, 128, NFB, 512], BF16)
        dbgt("dbg_k", [32, 128, L], BF16)
        dbgt("dbg_v", [NH, 8, 128, 4, HD], BF16)
        dbgt("dbg_qeff", [2, 128, 32 * 512], BF16)
        dbgt("dbg_attn", [2, 128, NFB * 512], F32)
        dbgt("dbg_ln1", [2, 128, NFB * 512], F32)
        dbgt("dbg_ffn", [2, 128, NFB * 512], F32)
        dbgt("dbg_bd", [128, 16 * 128], BF16)

    BASE0 = (nc._sbuf_addr_for_side("left") + 63) // 64 * 64

    def at(name, shape, dtype, off):
        return nc.alloc_sbuf_tensor_at(name, shape, dtype, offset=BASE0 + off).ap()

    o = [0]

    def nxt(sz):
        r = o[0]
        o[0] += (sz + 31) // 32 * 32
        return r

    ident_f32 = at("c_idf", [128, 128], F32, nxt(512))
    ident_bf16 = at("c_idb", [128, 128], BF16, nxt(256))
    ones_f32 = at("c_onf", [128, 128], F32, nxt(512))
    ones_bf16 = at("c_onb", [128, 128], BF16, nxt(256))
    d0g_sb = at("c_d0g", [128, 512], F32, nxt(2048))
    bq_sb = at("c_bq", [128, 32], F32, nxt(128))
    bk_sb = at("c_bk", [128, 32], F32, nxt(128))
    bv_sb = at("c_bv", [128, 32], F32, nxt(128))
    bb_sb = at("c_bias", [128, 32], F32, nxt(128))
    bw_sb = at("c_bw", [1, 48], F32, nxt(192))
    ws_sb = at("c_ws", [1, 1], F32, nxt(32))
    bwb_sb = at("c_bwb", [128, 48], F32, nxt(192))
    bd_sb = at("c_bd", [128, 16 * 128], BF16, nxt(4096))
    invd_sb = at("c_invd", [128, 1], BF16, nxt(32))
    eps_sb = at("c_eps", [1, 1], F32, nxt(32))
    cm15_sb = at("c_cm15", [1, 128], F32, nxt(512))
    stat_sb = at("c_stat", [128, 96], F32, nxt(384))
    am_sb = at("c_am", [128, 8], F32, nxt(32))
    off_rows = nxt(6144)
    rows_sb = at("c_rows", [1, 1536], F32, off_rows)
    off_bca = nxt(2048)
    bca_sb = at("c_bca", [128, 512], F32, off_bca)
    bcb_sb = at("c_bcb", [128, 512], F32, nxt(2048))
    off_tst = nxt(2048)
    tst_sb = at("c_tst", [128, 512], F32, off_tst)
    diag_sb = [at("c_diag0", [128, 8 * 128], BF16, nxt(2048)),
               at("c_diag1", [128, 8 * 128], BF16, off_tst)]
    PB = (o[0] + 63) // 64 * 64
    qstg_sb = [at("c_qstg0", [128, 512], BF16, PB + 160 * KB),
               at("c_qstg1", [128, 512], BF16, PB + 161 * KB)]
    tst2_sb = at("f_tst2", [128, 512], F32, PB + 162 * KB)
    rowB1_sb = at("f_rb1", [1, 512], F32, PB + 164 * KB)
    rowB2_sb = at("f_rb2", [1, 512], F32, PB + 166 * KB)
    mean_b = at("f_meanb", [128, 512], F32, PB + 168 * KB)
    imx_b = at("f_imxb", [128, 512], F32, PB + 170 * KB)
    amw_b = at("f_amwb", [128, 512], F32, PB + 172 * KB)

    NM, SM, S4c, T4c = 0, 8, 16, 24
    MNEG, TOT, INV = 32, 33, 34

    wkh_sb = [at(f"a_wk{i}", [128, 32 * 256], BF16, PB + i * 16 * KB) for i in range(2)]
    wvh_sb = [at(f"a_wv{i}", [128, 32 * 256], BF16, PB + 32 * KB + i * 16 * KB) for i in range(2)]
    xb_sb = [at(f"a_xb{i}", [128, NFB * 512], BF16, PB + 64 * KB + i * 32 * KB) for i in range(2)]
    kstg_sb = [at(f"a_ks{i}", [128, 2 * 512], BF16, PB + 128 * KB + i * 2 * KB) for i in range(3)]
    vstg_sb = [at(f"a_vs{i}", [128, 4 * 256], BF16, PB + 134 * KB + i * 2 * KB) for i in range(3)]
    qeff_sb = at("b_qeff", [128, 32 * 512], BF16, PB + 0)
    xs_sb = at("b_xs", [128, NFB * 512], F32, PB + 32 * KB)
    kh_sb = [at(f"b_kh{i}", [128, 2 * L], BF16, PB + 96 * KB + i * 16 * KB) for i in range(2)]
    vh_sb = [at(f"b_vh{i}", [128, 32 * HD], BF16, PB + 128 * KB + i * 16 * KB) for i in range(2)]
    p_sb = [at("b_p0", [128, L], BF16, PB + 160 * KB),
            at("b_p1", [128, L], BF16, off_rows)]
    pt_sb = at("b_pt", [128, 32 * 256], BF16, PB + 168 * KB)
    wqh_sb = [at(f"b_wq{i}", [128, 32 * 256], BF16, PB + 128 * KB + i * 16 * KB) for i in range(2)]
    xq_sb = at("b_xq", [128, NFB * 512], BF16, PB + 96 * KB)
    xbf_sb = at("f_xbf", [128, NFB * 512], BF16, PB + 0)
    sq_sb = at("f_sq", [128, NFB * 512], BF16, PB + 96 * KB)
    wfh_sb = [at(f"f_wf{i}", [128, 32 * 256], BF16, PB + 96 * KB + i * 16 * KB) for i in range(2)]
    xn_sb = at("f_xn", [128, NFB * 512], BF16, PB + 128 * KB)
    assert BASE0 + PB + 184 * KB <= nc._sbuf_addr_for_side('right'), (BASE0, PB)

    ps_a0 = nc.alloc_psum_tensor("ps_a0", [128, 512], F32).ap()
    ps_a1 = nc.alloc_psum_tensor("ps_a1", [128, 512], F32).ap()
    ps_b0 = nc.alloc_psum_tensor("ps_b0", [128, 512], F32).ap()
    ps_b1 = nc.alloc_psum_tensor("ps_b1", [128, 512], F32).ap()
    ps_c = nc.alloc_psum_tensor("ps_c", [128, 512], F32).ap()
    ps_d = nc.alloc_psum_tensor("ps_d", [128, 512], F32).ap()
    ps_e = nc.alloc_psum_tensor("ps_e", [128, 512], F32).ap()
    ps_f = nc.alloc_psum_tensor("ps_f", [128, 512], F32).ap()

    sLD, sCD, sST = PSem("sLD"), PSem("sCD"), PSem("sST")
    sPE, sACT, sDVE = PSem("sPE"), PSem("sACT"), PSem("sDVE")
    sPOOL = PSem("sPOOL")
    sSTG = PSem("sSTG")
    bsems = {e: PSem("sB_" + e) for e in ENGS}
    all_sems = [sLD, sCD, sST, sPE, sACT, sDVE, sPOOL, sSTG]

    pl = Plan()

    def barrier():
        targets = {s.name: s.n for s in all_sems}
        for e in ENGS:
            for s in all_sems:
                pl.wait(e, s, targets[s.name])
        for e in ENGS:
            b = bsems[e]
            b.n += 1
            pl.emit(e, lambda eng, s=b: eng.sem_inc(s.h, 1))
        for e in ENGS:
            for e2 in ENGS:
                pl.wait(e, bsems[e2], bsems[e2].n)

    ZONE = ["x"]

    def _pq(e):
        if PARANOID is True:
            return True
        if isinstance(PARANOID, (set, frozenset, list, tuple)):
            return ZONE[0] in PARANOID
        return False

    def _paranoid(e):
        if not _pq(e):
            return
        for s in all_sems:
            pl.wait(e, s, s.n)

    def dma(e, out, in_, sem=None):
        _paranoid(e)
        sem = sem or (sCD if e == "gpsimd" else sLD)
        sem.n += 16
        pl.emit(e, lambda eng, o_=out, i_=in_, s=sem: eng.dma_start(out=o_, in_=i_).then_inc(s.h, 16))
        return sem.n

    def mm(out, lhsT, rhs, start, stop, inc=False):
        _paranoid("tensor")
        if _pq("tensor"):
            inc = True
        if inc:
            sPE.n += 1
            pl.emit("tensor", lambda eng, o_=out, l_=lhsT, r_=rhs, st=start, sp=stop:
                    eng.matmul(o_, l_, r_, start=st, stop=sp).then_inc(sPE.h, 1))
            return sPE.n
        pl.emit("tensor", lambda eng, o_=out, l_=lhsT, r_=rhs, st=start, sp=stop:
                eng.matmul(o_, l_, r_, start=st, stop=sp))

    def ptrans(out, in_, ident, inc=False):
        _paranoid("tensor")
        if _pq("tensor"):
            inc = True
        if inc:
            sPE.n += 1
            pl.emit("tensor", lambda eng, o_=out, i_=in_, d_=ident:
                    eng.transpose(o_, i_, d_).then_inc(sPE.h, 1))
            return sPE.n
        pl.emit("tensor", lambda eng, o_=out, i_=in_, d_=ident: eng.transpose(o_, i_, d_))

    def act(out, in_, func=AF.Copy, bias=0.0, scale=1.0, accum=None, inc=False):
        _paranoid("scalar")
        if _pq("scalar"):
            inc = True
        if func == AF.Copy and not isinstance(bias, float):
            func = AF.Identity

        def f(eng, o_=out, i_=in_, fu=func, b_=bias, sc=scale, ac=accum):
            if ac is None:
                return eng.activation(o_, i_, fu, bias=b_, scale=sc)
            return eng.activation(o_, i_, fu, bias=b_, scale=sc, accum_out=ac)
        if inc:
            sACT.n += 1
            pl.emit("scalar", lambda eng, f_=f: f_(eng).then_inc(sACT.h, 1))
            return sACT.n
        pl.emit("scalar", f)

    def pop(builder, inc=False):
        _paranoid("gpsimd")
        if inc:
            sPOOL.n += 1
            pl.emit("gpsimd", lambda eng, b=builder: b(eng).then_inc(sPOOL.h, 1))
            return sPOOL.n
        pl.emit("gpsimd", builder)

    def vop(builder, inc=False):
        _paranoid("vector")
        if _pq("vector"):
            inc = True
        if inc:
            sDVE.n += 1
            pl.emit("vector", lambda eng, b=builder: b(eng).then_inc(sDVE.h, 1))
            return sDVE.n
        pl.emit("vector", builder)

    for dst, src in ((ident_f32[:, :], idf_in), (ident_bf16[:, :], idb_in),
                     (ones_f32[:, :], onf_in), (ones_bf16[:, :], onb_in),
                     (d0g_sb[:, :], d0g_in), (bq_sb[:, :], bq_in),
                     (bk_sb[:, :], bk_in), (bv_sb[:, :], bv_in),
                     (bb_sb[:, :], bb_in), (bw_sb[:, :], bw_in),
                     (ws_sb[:, :], ws_in), (invd_sb[:, :], invd_in),
                     (eps_sb[:, :], eps_in), (cm15_sb[:, :], cm15_in)):
        dma("sync", dst, src)
    v_consts = sLD.n

    pl.wait("tensor", sLD, v_consts)
    v_bwmm = mm(ps_c[:, 0:48], ones_f32[0:1, 0:128], bw_sb[0:1, 0:48], True, True, inc=True)
    pl.wait("scalar", sPE, v_bwmm)
    v_bwb = act(bwb_sb[:, :], ps_c[:, 0:48], inc=True)

    pl.wait("vector", sACT, v_bwb)
    pl.wait("vector", sLD, v_consts)
    v_bd = 0
    for h in range(16):
        g0, g1, g2 = d0g_sb[:, 128:256], d0g_sb[:, 256:384], d0g_sb[:, 384:512]
        d0 = d0g_sb[:, 0:128]
        acc = tst_sb[:, 0:128]
        s0 = bwb_sb[:, 3 * h:3 * h + 1]
        s1 = bwb_sb[:, 3 * h + 1:3 * h + 2]
        s2 = bwb_sb[:, 3 * h + 2:3 * h + 3]
        out = bd_sb[:, h * 128:(h + 1) * 128]
        vop(lambda e, a=acc, g=g0, s=s0: e.tensor_scalar_mul(a, g, s))
        vop(lambda e, a=acc, g=g1, s=s1: e.scalar_tensor_tensor(a, g, s, a, ALU.mult, ALU.add))
        vop(lambda e, a=acc, g=g2, s=s2: e.scalar_tensor_tensor(a, g, s, a, ALU.mult, ALU.add))
        v_bd = vop(lambda e, a=acc, d_=d0, o_=out: e.tensor_add(o_, a, d_), inc=True)

    if dbg:
        pl.wait("sync", sDVE, v_bd)
        dma("sync", dbg_outs["dbg_bd"], bd_sb[:, :], sST)

    ev_ps = {"c": v_bwb, "d": 0, "e": 0, "f": 0}
    st_kstg = [0, 0, 0]
    st_vstg = [0, 0, 0]
    v_xb = [(sLD, 0), (sLD, 0)]
    xb_st = [0, 0]
    v_xst = [0] * 8
    pe_xb_done = [0, 0]
    v_w = [0, 0]
    pe_w_done = [0, 0]

    def load_xchunk(ci):
        par = ci % 2
        lc = ci % 8
        if ci < 8:
            pl.wait("gpsimd", sPE, pe_xb_done[par])
            pl.wait("gpsimd", sST, xb_st[par])
            v_xb[par] = (sCD, dma("gpsimd", xb_sb[par][:, :],
                                  x_sw[lc].rearrange("p fb l -> p (fb l)")))
            pl.wait("sync", sCD, v_xb[par][1])
            v_xst[lc] = dma("sync", xbd[lc].rearrange("p fb l -> p (fb l)"), xb_sb[par][:, :], sST)
            xb_st[par] = v_xst[lc]
        else:
            pl.wait("sync", sPE, pe_xb_done[par])
            pl.wait("sync", sST, v_xst[lc])
            v_xb[par] = (sLD, dma("sync", xb_sb[par][:, :], xbd[lc].rearrange("p fb l -> p (fb l)")))

    def load_wpair(g):
        wpar = g % 2
        pl.wait("gpsimd", sPE, pe_w_done[wpar])
        dma("gpsimd", wkh_sb[wpar][:, :], wk_sw[g])
        v_w[wpar] = dma("gpsimd", wvh_sb[wpar][:, :], wv_sw[g])

    if upto != "none":
        load_xchunk(0)
        load_xchunk(1)
        load_wpair(0)
        load_wpair(1)
        for g in range(16):
            wpar = g % 2
            for lc in range(8):
                ci = g * 8 + lc
                par = ci % 2
                stg = ci % 3
                pl.wait("tensor", sCD, v_w[wpar])
                pl.wait("tensor", v_xb[par][0], v_xb[par][1])
                for ob in range(2):
                    bank = ("c", "d")[ob]
                    ps = (ps_c, ps_d)[ob]
                    pl.wait("tensor", sACT, ev_ps[bank])
                    for ib in range(32):
                        vv = mm(ps[:, 0:512],
                                wkh_sb[wpar][:, ib * 256 + ob * 128: ib * 256 + ob * 128 + 128],
                                xb_sb[par][:, ib * 512: ib * 512 + 512],
                                ib == 0, ib == 31, inc=(ib == 31))
                    pl.wait("scalar", sPE, vv)
                    pl.wait("scalar", sST, st_kstg[stg])
                    ev_ps[bank] = act(kstg_sb[stg][:, ob * 512:(ob + 1) * 512], ps[:, 0:512],
                                      AF.Copy, bias=bk_sb[:, g * 2 + ob: g * 2 + ob + 1],
                                      inc=True)
                pl.wait("sync", sACT, ev_ps["d"])
                st_kstg[stg] = dma(
                    "sync",
                    kd[g * 2:(g + 1) * 2, :, lc * 512:(lc + 1) * 512].rearrange("b p l -> p b l"),
                    kstg_sb[stg][:, :].rearrange("p (b l) -> p b l", b=2), sST)
                for lt in range(4):
                    bank = ("e", "f")[lt % 2]
                    ps = (ps_e, ps_f)[lt % 2]
                    pl.wait("tensor", sACT, ev_ps[bank])
                    for ib in range(32):
                        vv = mm(ps[:, 0:256],
                                xb_sb[par][:, ib * 512 + lt * 128: ib * 512 + lt * 128 + 128],
                                wvh_sb[wpar][:, ib * 256: ib * 256 + 256],
                                ib == 0, ib == 31, inc=(ib == 31))
                    pl.wait("scalar", sPE, vv)
                    pl.wait("scalar", sSTG, st_vstg[stg])
                    ev_ps[bank] = act(vstg_sb[stg][:, lt * 256:(lt + 1) * 256], ps[:, 0:256],
                                      inc=True)
                pl.wait("gpsimd", sACT, ev_ps[("e", "f")[3 % 2]])
                st_vstg[stg] = dma(
                    "gpsimd",
                    vd[g, lc],
                    vstg_sb[stg][:, :].rearrange("p (lt f) -> p lt f", lt=4), sSTG)
                pe_xb_done[par] = sPE.n
                if ci + 2 < 128:
                    load_xchunk(ci + 2)
            pe_w_done[wpar] = sPE.n
            if g + 2 < 16:
                load_wpair(g + 2)

    if dbg:
        pl.wait("sync", sST, sST.n)
        dma("sync", dbg_outs["dbg_xbd"], xbd, sST)
        dma("sync", dbg_outs["dbg_k"], kd, sST)
        dma("sync", dbg_outs["dbg_v"], vd, sST)

    barrier()
    if upto == "kv":
        dma("gpsimd", outT, xbd[0:2])
        for s in all_sems:
            pl.wait("sync", s, s.n)

    next_q = {}

    def q_prefetch(hf, pe_floor_xq, pe_floor_wq):
        pl.wait("sync", sPE, pe_floor_xq)
        v_xq = dma("sync", xq_sb[:, :], xbd[hf].rearrange("p fb l -> p (fb l)"))
        pl.wait("gpsimd", sPE, pe_floor_wq)
        w0 = dma("gpsimd", wqh_sb[0][:, :], wq_sw[0])
        w1 = dma("gpsimd", wqh_sb[1][:, :], wq_sw[1])
        next_q[hf] = (v_xq, [w0, w1])

    def half_program(hf):
        if hf in next_q:
            v_xq, v_wq = next_q.pop(hf)
        else:
            v_xq = dma("sync", xq_sb[:, :], xbd[hf].rearrange("p fb l -> p (fb l)"))
            v_wq = [0, 0]
        pe_wq_done = [0, 0]
        q_ev = [0, 0]
        qe_ev = {"c": 0, "d": 0}
        bl_mm = [0, 0]
        qe_ready = 0

        def load_wq(g):
            wpar = g % 2
            pl.wait("gpsimd", sPE, pe_wq_done[wpar])
            v_wq[wpar] = dma("gpsimd", wqh_sb[wpar][:, :], wq_sw[g])

        if v_wq[0] == 0:
            load_wq(0)
            load_wq(1)
        for g in range(16):
            wpar = g % 2
            pl.wait("tensor", sCD, v_wq[wpar])
            pl.wait("tensor", sLD, v_xq)
            for ob in range(2):
                obg = g * 2 + ob
                ps = (ps_a0, ps_a1)[ob]
                pl.wait("tensor", sACT, q_ev[ob])
                for ib in range(32):
                    vq = mm(ps[:, :],
                            wqh_sb[wpar][:, ib * 256 + ob * 128: ib * 256 + ob * 128 + 128],
                            xq_sb[:, ib * 512: ib * 512 + 512],
                            ib == 0, ib == 31, inc=(ib == 31))
                pl.wait("scalar", sPE, vq)
                pl.wait("scalar", sPE, bl_mm[ob])
                q_ev[ob] = act(qstg_sb[ob][:, :], ps[:, :], AF.Copy,
                               bias=bq_sb[:, obg:obg + 1], inc=True)
            for ob in range(2):
                obg = g * 2 + ob
                h = obg // 2
                cb = ("c", "d")[ob]
                psb = (ps_c, ps_d)[ob]
                pl.wait("tensor", sACT, q_ev[ob])
                pl.wait("tensor", sACT, qe_ev[cb])
                bl_mm[ob] = mm(psb[:, 0:512], bd_sb[:, h * 128:(h + 1) * 128],
                               qstg_sb[ob][:, :], True, True, inc=True)
                pl.wait("scalar", sPE, bl_mm[ob])
                qe_ev[cb] = act(qeff_sb[:, obg * 512:(obg + 1) * 512], psb[:, 0:512], inc=True)
                qe_ready = qe_ev[cb]
            pe_wq_done[wpar] = sPE.n
            if g + 2 < 16:
                load_wq(g + 2)
        if dbg:
            pl.wait("sync", sACT, qe_ready)
            dma("sync", dbg_outs["dbg_qeff"][hf], qeff_sb[:, :], sST)

        v_xs = dma("sync", xs_sb[:, :], x_sw[hf].rearrange("p fb l -> p (fb l)"))
        barrier()
        if upto == "q":
            return

        v_kh = [0, 0]
        v_vh = [0, 0]
        kh_free = [0, 0]
        s_bufs = (ps_a0, ps_b0, ps_a1)
        s_bank_ev = [0, 0, 0]
        s_rot = [0]
        bank_act = {"c": 0, "d": 0, "e": 0, "f": 0, "g": 0}
        bank_dve = {"c": 0, "d": 0, "e": 0, "f": 0, "g": 0}
        PSB = {"c": ps_c, "d": ps_d, "e": ps_e, "f": ps_f, "g": ps_b1}
        v_pt_last = [0, 0]
        v_av_last = 0
        ev_p_qt = [0, 0]
        st_done = [0, 0]
        av_ready_a = [0, 0]
        av_ready_d = [0, 0]
        av_fin = [0] * NH
        qeff_fin = [0] * NH
        pt4 = pt_sb[:, :].rearrange("p (q mtl) -> p q mtl", q=2)
        v_kh[0] = dma("sync", kh_sb[0][:, :].rearrange("p (b m) -> p b m", b=2),
                      kd[0:2].rearrange("b p m -> p b m"))
        v_vh[0] = dma("sync", vh_sb[0][:, :].rearrange("p (lc x) -> p lc x", lc=8),
                      vd[0].rearrange("lc p lt f -> p lc (lt f)"))
        pending = None

        for h in range(NH):
            par = h % 2

            def S_qt(qt):
                ZONE[0] = "S"
                pq = qt % 2
                sb_off = 48 * pq
                for ck in range(8):
                    sb = s_rot[0] % 3
                    s_rot[0] += 1
                    ps = s_bufs[sb]
                    pl.wait("tensor", sACT, s_bank_ev[sb])
                    for fb2 in range(2):
                        vS = mm(ps[:, :],
                                qeff_sb[:, (2 * h + fb2) * 512 + qt * 128:
                                        (2 * h + fb2) * 512 + qt * 128 + 128],
                                kh_sb[par][:, fb2 * L + ck * 512:
                                           fb2 * L + ck * 512 + 512],
                                fb2 == 0, fb2 == 1, inc=(fb2 == 1))
                    pl.wait("vector", sPE, vS)
                    v_nm = vop(lambda e, ps_=ps, c_=ck + sb_off:
                               e.reduce_max(stat_sb[:, NM + c_:NM + c_ + 1], ps_[:, :],
                                            axis=AX.X, negate=True), inc=True)
                    pl.wait("scalar", sDVE, v_nm)
                    pl.wait("scalar", sPE, v_pt_last[pq])
                    s_bank_ev[sb] = act(p_sb[pq][:, ck * 512:(ck + 1) * 512], ps[:, :],
                                        AF.Exp,
                                        bias=stat_sb[:, NM + sb_off + ck:NM + sb_off + ck + 1],
                                        scale=1.0,
                                        accum=stat_sb[:, SM + sb_off + ck:SM + sb_off + ck + 1],
                                        inc=True)
                ev_p_qt[pq] = s_bank_ev[(s_rot[0] - 1) % 3]

            def ST_qt(qt):
                ZONE[0] = "stats"
                pq = qt % 2
                c0 = 48 * pq
                pl.wait("vector", sACT, ev_p_qt[pq])
                pl.wait("vector", sPE, v_pt_last[pq])
                v1 = vop(lambda e: e.tensor_reduce(stat_sb[:, c0 + MNEG:c0 + MNEG + 1],
                                                   stat_sb[:, c0 + NM:c0 + NM + 8],
                                                   AX.X, ALU.min), inc=True)
                pl.wait("scalar", sDVE, v1)
                v3 = act(stat_sb[:, c0 + S4c:c0 + S4c + 8], stat_sb[:, c0 + NM:c0 + NM + 8],
                         AF.Exp, bias=stat_sb[:, c0 + MNEG:c0 + MNEG + 1], scale=-1.0, inc=True)
                pl.wait("vector", sACT, v3)
                v4 = vop(lambda e: e.scalar_tensor_tensor(
                    stat_sb[:, c0 + T4c:c0 + T4c + 8], stat_sb[:, c0 + S4c:c0 + S4c + 8],
                    1.0, stat_sb[:, c0 + SM:c0 + SM + 8], ALU.mult, ALU.mult,
                    accum_out=stat_sb[:, c0 + TOT:c0 + TOT + 1]), inc=True)
                pl.wait("vector", sDVE, v4)
                v5 = vop(lambda e: e.reciprocal(stat_sb[:, c0 + INV:c0 + INV + 1],
                                                stat_sb[:, c0 + TOT:c0 + TOT + 1]), inc=True)
                pl.wait("vector", sDVE, v5)
                vd_ = 0
                for ck in range(8):
                    vd_ = vop(lambda e, c_=ck:
                              e.tensor_scalar(diag_sb[pq][:, c_ * 128:(c_ + 1) * 128],
                                              ident_bf16[:, :],
                                              stat_sb[:, c0 + S4c + c_:c0 + S4c + c_ + 1],
                                              stat_sb[:, c0 + INV:c0 + INV + 1],
                                              ALU.mult, ALU.mult), inc=True)
                st_done[pq] = vd_

            def PT_qt(qt):
                ZONE[0] = "PT"
                pq = qt % 2
                qtl = qt % 2
                pl.wait("tensor", sDVE, st_done[pq])
                pl.wait("tensor", sACT, ev_p_qt[pq])
                last_a, last_d = 0, 0
                for r in range(8):
                    bk = ("c", "d", "f", "g")[r % 4]
                    ps = PSB[bk]
                    pl.wait("tensor", sACT, bank_act[bk])
                    pl.wait("tensor", sDVE, bank_dve[bk])
                    for j in range(4):
                        mt = 4 * r + j
                        vPT = mm(ps[:, j * 128:(j + 1) * 128],
                                 p_sb[pq][:, mt * 128:(mt + 1) * 128],
                                 diag_sb[pq][:, (mt // 4) * 128:(mt // 4) * 128 + 128],
                                 True, True, inc=(j == 3))
                    v_pt_last[pq] = vPT
                    dst = pt_sb[:, qtl * 4096 + r * 512: qtl * 4096 + (r + 1) * 512]
                    if r % 2 == 0:
                        pl.wait("scalar", sPE, vPT)
                        pl.wait("scalar", sPE, v_av_last)
                        last_a = act(dst, ps[:, :], inc=True)
                        bank_act[bk] = last_a
                    else:
                        pl.wait("vector", sPE, vPT)
                        pl.wait("vector", sPE, v_av_last)
                        last_d = vop(lambda e, d_=dst, ps_=ps: e.tensor_copy(d_, ps_[:, :]),
                                     inc=True)
                        bank_dve[bk] = last_d
                av_ready_a[qt // 2] = last_a
                av_ready_d[qt // 2] = last_d

            def AV_qp(qp, hh, pp):
                nonlocal v_av_last
                ZONE[0] = "AV"
                pl.wait("tensor", sACT, av_ready_a[qp])
                pl.wait("tensor", sDVE, av_ready_d[qp])
                pl.wait("tensor", sLD, v_vh[pp])
                pl.wait("tensor", sDVE, bank_dve["e"])
                for fb2 in range(2):
                    for mt in range(32):
                        vAV = mm(ps_e[:, fb2 * 256:(fb2 + 1) * 256],
                                 vh_sb[pp][:, mt * HD + fb2 * 128: mt * HD + fb2 * 128 + 128],
                                 pt4[:, :, mt * 128:(mt + 1) * 128],
                                 mt == 0, mt == 31, inc=(mt == 31))
                v_av_last = vAV
                pl.wait("vector", sPE, vAV)
                pl.wait("vector", sLD, v_xs)
                for fb2 in range(2):
                    fb = 2 * hh + fb2
                    xsl = xs_sb[:, fb * 512 + qp * 256: fb * 512 + qp * 256 + 256]
                    bank_dve["e"] = vop(
                        lambda e, s_=ps_e[:, fb2 * 256:(fb2 + 1) * 256], x_=xsl,
                        b_=bv_sb[:, fb:fb + 1]:
                        e.scalar_tensor_tensor(x_, s_, b_, x_, ALU.add, ALU.add),
                        inc=True)
                if qp == 1:
                    av_fin[hh] = bank_dve["e"]

            pl.wait("tensor", sLD, v_kh[par])
            S_qt(0)
            ST_qt(0)
            S_qt(1)
            if pending is not None:
                AV_qp(*pending)
                pending = None
            if h + 1 < NH:
                np_ = (h + 1) % 2
                kh_free[np_] = sPE.n
                pl.wait("sync", sPE, kh_free[np_])
                v_kh[np_] = dma("sync", kh_sb[np_][:, :].rearrange("p (b m) -> p b m", b=2),
                                kd[2 * (h + 1):2 * (h + 1) + 2].rearrange("b p m -> p b m"))
                v_vh[np_] = dma("sync", vh_sb[np_][:, :].rearrange("p (lc x) -> p lc x", lc=8),
                                vd[h + 1].rearrange("lc p lt f -> p lc (lt f)"))
            ST_qt(1)
            PT_qt(0)
            PT_qt(1)
            S_qt(2)
            ST_qt(2)
            S_qt(3)
            qeff_fin[h] = sPE.n
            AV_qp(0, h, par)
            ST_qt(3)
            PT_qt(2)
            PT_qt(3)
            pending = (1, h, par)
            ZONE[0] = "x"
        AV_qp(*pending)
        pending = None
        attn_pe_end = sPE.n
        attn_marks = [(sDVE, bank_dve["e"]), (sACT, bank_act["c"]), (sACT, bank_act["d"]),
                      (sACT, bank_act["f"]), (sDVE, bank_dve["c"]), (sDVE, bank_dve["d"]),
                      (sDVE, bank_dve["f"]), (sACT, s_bank_ev[0]), (sACT, s_bank_ev[1]),
                      (sACT, s_bank_ev[2]), (sACT, bank_act["g"]), (sDVE, bank_dve["g"])]
        if dbg:
            pl.wait("sync", sDVE, sDVE.n)
            dma("sync", dbg_outs["dbg_attn"][hf], xs_sb[:, :], sST)
            barrier()
        if upto == "attn":
            pl.wait("sync", sDVE, sDVE.n)
            dma("sync", outT[hf], xs_sb[:, :].rearrange("p (fb l) -> p fb l", fb=NFB), sST)
            barrier()
            return

        def layer_norm(dump_key, cast_wait=None, sq_wait=None, bc_wait=None,
                       psum_waits=()):
            vcast, vsqv = [0] * NFB, [0] * NFB
            for fb in range(NFB):
                if cast_wait is not None:
                    cast_wait(fb)
                vcast[fb] = pop(lambda e, f_=fb: e.tensor_copy(
                    xbf_sb[:, f_ * 512:(f_ + 1) * 512], xs_sb[:, f_ * 512:(f_ + 1) * 512]),
                    inc=True)
            if sq_wait is not None:
                sq_wait()
            for fb in range(NFB):
                pl.wait("vector", sPOOL, vcast[fb])
                vsqv[fb] = vop(lambda e, f_=fb: e.tensor_mul(
                    sq_sb[:, f_ * 512:(f_ + 1) * 512], xbf_sb[:, f_ * 512:(f_ + 1) * 512],
                    xbf_sb[:, f_ * 512:(f_ + 1) * 512]), inc=True)
            for sm_, val_ in psum_waits:
                pl.wait("tensor", sm_, val_)
            for fb in range(NFB):
                pl.wait("tensor", sPOOL, vcast[fb])
                mm(ps_e[0:1, 0:512], invd_sb[:, 0:1],
                   xbf_sb[:, fb * 512:(fb + 1) * 512], fb == 0, fb == NFB - 1,
                   inc=(fb == NFB - 1))
                pl.wait("tensor", sDVE, vsqv[fb])
                vsq = mm(ps_f[0:1, 0:512], invd_sb[:, 0:1],
                         sq_sb[:, fb * 512:(fb + 1) * 512], fb == 0, fb == NFB - 1,
                         inc=(fb == NFB - 1))
            pe_sums = vsq
            pl.wait("scalar", sPE, vsq)
            act(rows_sb[0:1, 0:512], ps_e[0:1, 0:512])
            v_rows = act(rows_sb[0:1, 512:1024], ps_f[0:1, 0:512], inc=True)
            pl.wait("vector", sACT, v_rows)
            v_m2 = vop(lambda e: e.tensor_mul(rows_sb[0:1, 1024:1536], rows_sb[0:1, 0:512],
                                              rows_sb[0:1, 0:512]), inc=True)
            pl.wait("vector", sDVE, v_m2)
            v_var = vop(lambda e: e.scalar_tensor_tensor(
                rows_sb[0:1, 512:1024], rows_sb[0:1, 1024:1536], -1.0,
                rows_sb[0:1, 512:1024], ALU.mult, ALU.add), inc=True)
            pl.wait("scalar", sDVE, v_var)
            act(rows_sb[0:1, 512:1024], rows_sb[0:1, 512:1024], AF.Ln,
                bias=eps_sb[0:1, 0:1], scale=1.0)
            v_rstd = act(rows_sb[0:1, 512:1024], rows_sb[0:1, 512:1024], AF.Exp,
                         bias=0.0, scale=-0.5, inc=True)
            pl.wait("vector", sACT, v_rstd)
            v_mr = vop(lambda e: e.scalar_tensor_tensor(
                rows_sb[0:1, 0:512], rows_sb[0:1, 0:512], -1.0,
                rows_sb[0:1, 512:1024], ALU.mult, ALU.mult), inc=True)
            pl.wait("tensor", sDVE, v_mr)
            mm(ps_c[:, 0:512], ones_f32[0:1, 0:128], rows_sb[0:1, 512:1024], True, True, inc=True)
            v_b2 = mm(ps_d[:, 0:512], ones_f32[0:1, 0:128], rows_sb[0:1, 0:512], True, True, inc=True)
            pl.wait("scalar", sPE, v_b2)
            if bc_wait is not None:
                bc_wait()
            act(bca_sb[:, :], ps_c[:, 0:512])
            v_bc = act(bcb_sb[:, :], ps_d[:, 0:512], inc=True)
            pl.wait("vector", sACT, v_bc)
            pl.wait("gpsimd", sACT, v_bc)
            nfin = [None] * NFB
            v_ln_d = v_ln_p = 0
            for fb in range(NFB):
                sl = xs_sb[:, fb * 512:(fb + 1) * 512]
                if fb < 20:
                    vop(lambda e, s_=sl: e.tensor_mul(s_, s_, bca_sb[:, :]))
                    v_ln_d = vop(lambda e, s_=sl: e.tensor_add(s_, s_, bcb_sb[:, :]), inc=True)
                    nfin[fb] = (sDVE, v_ln_d)
                else:
                    pop(lambda e, s_=sl: e.tensor_mul(s_, s_, bca_sb[:, :]))
                    v_ln_p = pop(lambda e, s_=sl: e.tensor_add(s_, s_, bcb_sb[:, :]), inc=True)
                    nfin[fb] = (sPOOL, v_ln_p)
            if dbg and dump_key is not None:
                pl.wait("sync", sDVE, v_ln_d)
                pl.wait("sync", sPOOL, v_ln_p)
                dma("sync", dbg_outs[dump_key][hf], xs_sb[:, :], sST)
            return dict(nfin=nfin, pe_sums=pe_sums, d=v_ln_d, p=v_ln_p)

        vcast, vsqv = [0] * NFB, [0] * NFB
        for fb in range(NFB):
            pl.wait("gpsimd", sDVE, av_fin[fb // 2])
            pl.wait("gpsimd", sPE, qeff_fin[fb // 2])
            vcast[fb] = pop(lambda e, f_=fb: e.tensor_copy(
                xbf_sb[:, f_ * 512:(f_ + 1) * 512], xs_sb[:, f_ * 512:(f_ + 1) * 512]),
                inc=True)
        pl.wait("vector", sPE, qeff_fin[NH - 1])
        for fb in range(NFB):
            pl.wait("vector", sPOOL, vcast[fb])
            vsqv[fb] = vop(lambda e, f_=fb: e.tensor_mul(
                sq_sb[:, f_ * 512:(f_ + 1) * 512], xbf_sb[:, f_ * 512:(f_ + 1) * 512],
                xbf_sb[:, f_ * 512:(f_ + 1) * 512]), inc=True)
        for sm_, val_ in attn_marks:
            pl.wait("tensor", sm_, val_)
        for fb in range(NFB):
            pl.wait("tensor", sPOOL, vcast[fb])
            mm(ps_e[0:1, 0:512], invd_sb[:, 0:1],
               xbf_sb[:, fb * 512:(fb + 1) * 512], fb == 0, fb == NFB - 1,
               inc=(fb == NFB - 1))
            pl.wait("tensor", sDVE, vsqv[fb])
            vsq = mm(ps_f[0:1, 0:512], invd_sb[:, 0:1],
                     sq_sb[:, fb * 512:(fb + 1) * 512], fb == 0, fb == NFB - 1,
                     inc=(fb == NFB - 1))
        ln1_pe_sums = vsq
        tr_dve = [0, 0]
        trg_fin = [0] * 8
        v_am = 0
        for lt in range(4):
            for r in range(8):
                sbk = r % 2
                ps = (ps_a0, ps_a1)[sbk]
                pl.wait("tensor", sDVE, tr_dve[sbk])
                for j in range(4):
                    fb = r * 4 + j
                    pl.wait("tensor", sDVE, av_fin[fb // 2])
                    vT = ptrans(ps[:, j * 128:(j + 1) * 128],
                                xs_sb[:, fb * 512 + lt * 128: fb * 512 + lt * 128 + 128],
                                ident_f32[:, :], inc=(j == 3))
                if lt == 3:
                    trg_fin[r] = vT
                pl.wait("vector", sPE, vT)
                vop(lambda e, ps_=ps, r_=r:
                    e.reduce_max(stat_sb[:, r_:r_ + 1], ps_[:, :], axis=AX.X))
                tr_dve[sbk] = vop(lambda e, ps_=ps, r_=r:
                                  e.tensor_reduce(stat_sb[:, 8 + r_:9 + r_], ps_[:, :],
                                                  AX.X, ALU.min), inc=True)
            pl.wait("vector", sDVE, tr_dve[0])
            pl.wait("vector", sDVE, tr_dve[1])
            vop(lambda e, l_=lt: e.tensor_reduce(am_sb[:, l_:l_ + 1],
                                                 stat_sb[:, 0:8], AX.X, ALU.max))
            v_am = vop(lambda e, l_=lt: e.tensor_reduce(am_sb[:, 4 + l_:5 + l_],
                                                        stat_sb[:, 8:16], AX.X, ALU.min),
                       inc=True)
        pl.wait("scalar", sPE, ln1_pe_sums)
        act(rows_sb[0:1, 0:512], ps_e[0:1, 0:512])
        v_rows = act(rows_sb[0:1, 512:1024], ps_f[0:1, 0:512], inc=True)
        pl.wait("tensor", sACT, v_rows)
        v_mb = mm(ps_c[:, 0:512], ones_f32[0:1, 0:128], rows_sb[0:1, 0:512], True, True,
                  inc=True)
        pl.wait("scalar", sPE, v_mb)
        v_mbe = act(mean_b[:, :], ps_c[:, 0:512], inc=True)
        pl.wait("tensor", sDVE, v_am)
        for lt in range(4):
            ptrans(ps_e[0:1, lt * 128:(lt + 1) * 128], am_sb[:, lt:lt + 1], ident_f32[:, :])
            vT2 = ptrans(ps_f[0:1, lt * 128:(lt + 1) * 128], am_sb[:, 4 + lt:5 + lt],
                         ident_f32[:, :], inc=(lt == 3))
        pl.wait("scalar", sPE, vT2)
        act(rowB1_sb[0:1, :], ps_e[0:1, 0:512])
        v_rB = act(rowB2_sb[0:1, :], ps_f[0:1, 0:512], inc=True)
        pl.wait("vector", sACT, v_rB)
        vop(lambda e: e.tensor_sub(rowB1_sb[0:1, :], rowB1_sb[0:1, :], rows_sb[0:1, 0:512]))
        vop(lambda e: e.tensor_sub(rowB2_sb[0:1, :], rows_sb[0:1, 0:512], rowB2_sb[0:1, :]))
        vop(lambda e: e.tensor_max(rowB2_sb[0:1, :], rowB2_sb[0:1, :], rowB1_sb[0:1, :]))
        v_mx = vop(lambda e: e.tensor_scalar_max(rowB2_sb[0:1, :], rowB2_sb[0:1, :], 1e-30),
                   inc=True)
        pl.wait("vector", sDVE, v_mx)
        v_imx = vop(lambda e: e.reciprocal(rowB1_sb[0:1, :], rowB2_sb[0:1, :]), inc=True)
        pl.wait("tensor", sDVE, v_imx)
        v_ib = mm(ps_d[:, 0:512], ones_f32[0:1, 0:128], rowB1_sb[0:1, :], True, True,
                  inc=True)
        pl.wait("scalar", sPE, v_ib)
        v_ibe = act(imx_b[:, :], ps_d[:, 0:512], inc=True)
        pl.wait("vector", sACT, v_ibe)
        pl.wait("gpsimd", sACT, v_ibe)
        pl.wait("vector", sPOOL, vcast[NFB - 1])
        vxn_d, vxn_p = [0] * NFB, [0] * NFB
        for fb in range(NFB):
            sl = xs_sb[:, fb * 512:(fb + 1) * 512]
            xn = xn_sb[:, fb * 512:(fb + 1) * 512]
            if fb % 8 < 6:
                vop(lambda e, x_=xn, s_=sl: e.tensor_sub(x_, s_, mean_b[:, :]))
                vxn_d[fb] = vop(lambda e, x_=xn: e.tensor_mul(x_, x_, imx_b[:, :]),
                                inc=True)
            else:
                pop(lambda e, x_=xn, s_=sl: e.tensor_sub(x_, s_, mean_b[:, :]))
                vxn_p[fb] = pop(lambda e, x_=xn: e.tensor_mul(x_, x_, imx_b[:, :]),
                                inc=True)
        for fb in range(NFB):
            pl.wait("tensor", sDVE, vxn_d[fb])
            pl.wait("tensor", sPOOL, vxn_p[fb])
            vxs = mm(ps_e[0:1, 0:512], ones_bf16[:, 0:1],
                     xn_sb[:, fb * 512:(fb + 1) * 512], fb == 0, fb == NFB - 1,
                     inc=(fb == NFB - 1))
        pl.wait("scalar", sPE, vxs)
        v_xnsum = act(rows_sb[0:1, 1024:1536], ps_e[0:1, 0:512], inc=True)
        pl.wait("tensor", sACT, v_xnsum)
        pl.wait("vector", sACT, v_rows)
        v_m2 = vop(lambda e: e.tensor_mul(tst_sb[0:1, 0:512], rows_sb[0:1, 0:512],
                                          rows_sb[0:1, 0:512]), inc=True)
        pl.wait("vector", sDVE, v_m2)
        v_var = vop(lambda e: e.scalar_tensor_tensor(
            rows_sb[0:1, 512:1024], tst_sb[0:1, 0:512], -1.0,
            rows_sb[0:1, 512:1024], ALU.mult, ALU.add), inc=True)
        pl.wait("scalar", sDVE, v_var)
        act(rows_sb[0:1, 512:1024], rows_sb[0:1, 512:1024], AF.Ln,
            bias=eps_sb[0:1, 0:1], scale=1.0)
        v_rstd = act(rows_sb[0:1, 512:1024], rows_sb[0:1, 512:1024], AF.Exp,
                     bias=0.0, scale=-0.5, inc=True)
        pl.wait("vector", sACT, v_rstd)
        vop(lambda e: e.tensor_mul(rowB2_sb[0:1, :], rowB2_sb[0:1, :],
                                   rows_sb[0:1, 512:1024]))
        vop(lambda e: e.tensor_scalar_max(rowB2_sb[0:1, :], rowB2_sb[0:1, :], FFN_EPS))
        v_amw = vop(lambda e: e.tensor_scalar_mul(rowB2_sb[0:1, :], rowB2_sb[0:1, :],
                                                  ws_sb[0:1, 0:1]), inc=True)
        v_mr = vop(lambda e: e.scalar_tensor_tensor(
            rows_sb[0:1, 0:512], rows_sb[0:1, 0:512], -1.0,
            rows_sb[0:1, 512:1024], ALU.mult, ALU.mult), inc=True)
        pl.wait("tensor", sDVE, v_mr)
        pl.wait("tensor", sDVE, v_amw)
        mm(ps_e[:, 0:512], ones_f32[0:1, 0:128], rows_sb[0:1, 512:1024], True, True, inc=True)
        mm(ps_f[:, 0:512], ones_f32[0:1, 0:128], rows_sb[0:1, 0:512], True, True, inc=True)
        v_b3 = mm(ps_b0[:, 0:512], ones_f32[0:1, 0:128], rowB2_sb[0:1, :], True, True, inc=True)
        pl.wait("scalar", sPE, v_b3)
        act(bca_sb[:, :], ps_e[:, 0:512])
        act(bcb_sb[:, :], ps_f[:, 0:512])
        v_bc = act(amw_b[:, :], ps_b0[:, 0:512], inc=True)
        pl.wait("vector", sACT, v_bc)
        pl.wait("gpsimd", sACT, v_bc)
        nfin = [None] * NFB
        v_ln_d = v_ln_p = 0
        for fb in range(NFB):
            sl = xs_sb[:, fb * 512:(fb + 1) * 512]
            pl.wait("vector" if fb < 20 else "gpsimd", sPE, trg_fin[fb // 4])
            if fb < 20:
                pl.wait("vector", sDVE, vxn_d[fb])
                pl.wait("vector", sPOOL, vxn_p[fb])
                vop(lambda e, s_=sl: e.tensor_mul(s_, s_, bca_sb[:, :]))
                v_ln_d = vop(lambda e, s_=sl: e.tensor_add(s_, s_, bcb_sb[:, :]), inc=True)
                nfin[fb] = (sDVE, v_ln_d)
            else:
                pl.wait("gpsimd", sDVE, vxn_d[fb])
                pop(lambda e, s_=sl: e.tensor_mul(s_, s_, bca_sb[:, :]))
                v_ln_p = pop(lambda e, s_=sl: e.tensor_add(s_, s_, bcb_sb[:, :]), inc=True)
                nfin[fb] = (sPOOL, v_ln_p)
        ln1 = dict(nfin=nfin, pe_sums=ln1_pe_sums, d=v_ln_d, p=v_ln_p)
        if dbg:
            pl.wait("sync", sDVE, v_ln_d)
            pl.wait("sync", sPOOL, v_ln_p)
            dma("sync", dbg_outs["dbg_ln1"][hf], xs_sb[:, :], sST)
        if upto == "ln1":
            pl.wait("sync", sDVE, ln1["d"])
            pl.wait("sync", sPOOL, ln1["p"])
            dma("sync", outT[hf], xs_sb[:, :].rearrange("p (fb l) -> p fb l", fb=NFB), sST)
            barrier()
            return

        v_wf = [0, 0]
        pe_wf_done = [0, 0]
        ffn_fin = [0] * NFB
        bank_free = {"c": 0, "d": 0}
        tst_free = [("d", 0), ("d", 0)]
        v_ffn_d = 0

        def load_wf(g):
            wpar = g % 2
            pl.wait("sync", sPE, max(pe_wf_done[wpar], ln1["pe_sums"]))
            v_wf[wpar] = dma("sync", wfh_sb[wpar][:, :], wf_sw[g], sLD)

        load_wf(0)
        load_wf(1)
        pl.wait("tensor", sACT, v_ibe)
        for g in range(16):
            wpar = g % 2
            pl.wait("tensor", sLD, v_wf[wpar])
            for ob in range(2):
                obg = g * 2 + ob
                cb = ("c", "d")[ob]
                ps = (ps_c, ps_d)[ob]
                pl.wait("tensor", sDVE, bank_free[cb])
                for ib in range(32):
                    mm(ps[:, 0:512],
                       wfh_sb[wpar][:, ib * 256 + ob * 128: ib * 256 + ob * 128 + 128],
                       xn_sb[:, ib * 512: ib * 512 + 512],
                       ib == 0, False, inc=False)
                vF = mm(ps[:, 0:512], cm15_sb[0:1, 0:128], rows_sb[0:1, 1024:1536],
                        False, True, inc=True)
                pl.wait("vector", sPE, vF)
                eng_prev, val_prev = tst_free[ob]
                pl.wait("vector", sDVE if eng_prev == "d" else sPOOL, val_prev)
                tp = (tst_sb, tst2_sb)[ob]
                vmul = vop(lambda e, ps_=ps, t_=tp: e.tensor_mul(t_[:, :], ps_[:, 0:512],
                                                                 amw_b[:, :]), inc=True)
                bank_free[cb] = vmul
                xsl = xs_sb[:, obg * 512:(obg + 1) * 512]
                sm_, val_ = nfin[obg]
                pl.wait("vector", sm_, val_)
                v_ffn_d = vop(
                    lambda e, x_=xsl, t_=tp, b_=bb_sb[:, obg:obg + 1]:
                    e.scalar_tensor_tensor(x_, t_[:, :], b_, x_, ALU.add, ALU.add),
                    inc=True)
                ffn_fin[obg] = v_ffn_d
                tst_free[ob] = ("d", v_ffn_d)
            pe_wf_done[wpar] = sPE.n
            if g + 2 < 16:
                load_wf(g + 2)
        ffn_pe_end = sPE.n
        ffn_dve_end = sDVE.n
        if dbg:
            pl.wait("sync", sDVE, sDVE.n)
            dma("sync", dbg_outs["dbg_ffn"][hf], xs_sb[:, :], sST)
            barrier()
        if upto == "ffn":
            pl.wait("sync", sDVE, sDVE.n)
            dma("sync", outT[hf], xs_sb[:, :].rearrange("p (fb l) -> p fb l", fb=NFB), sST)
            barrier()
            return

        ln2 = layer_norm(
            None,
            cast_wait=lambda fb: pl.wait("gpsimd", sDVE, ffn_fin[fb]),
            sq_wait=lambda: pl.wait("vector", sPE, ffn_pe_end),
            bc_wait=lambda: (pl.wait("scalar", sDVE, ffn_dve_end)),
            psum_waits=[(sACT, v_xnsum), (sACT, v_bc),
                        (sDVE, bank_free["c"]), (sDVE, bank_free["d"])])
        for gq in range(4):
            for fb in range(gq * 8, gq * 8 + 8):
                sm, val = ln2["nfin"][fb]
                pl.wait("sync", sm, val)
            dma("sync", outT[hf, :, gq * 8:(gq + 1) * 8, :],
                xs_sb[:, gq * 8 * 512:(gq + 1) * 8 * 512].rearrange(
                    "p (fb l) -> p fb l", fb=8), sST)
        if hf == 0:
            q_prefetch(1, ln2["pe_sums"], ffn_pe_end)
        barrier()

    if upto != "kv":
        for hf in range(2):
            half_program(hf)

    for s in all_sems:
        pl.wait("sync", s, s.n)

    import contextlib
    st = contextlib.ExitStack()
    for s in all_sems + list(bsems.values()):
        s.h = st.enter_context(nc.semaphore(s.name))
    with st:
        with nc.Block() as blk:
            blk.tensor(lambda eng: [f(eng) for f in pl.q["tensor"]])
            blk.scalar(lambda eng: [f(eng) for f in pl.q["scalar"]])
            blk.vector(lambda eng: [f(eng) for f in pl.q["vector"]])
            blk.gpsimd(lambda eng: [f(eng) for f in pl.q["gpsimd"]])
            blk.sync(lambda eng: [f(eng) for f in pl.q["sync"]])
    return nc


def _prep_inputs(L0_attn_x, L0_attn_Wq, L0_attn_bq, L0_attn_Wk, L0_attn_bk,
                 L0_attn_Wv, L0_attn_bv, L0_attn_bias_weight,
                 L0_ffn_weight_2bit, L0_ffn_weight_scale, L0_ffn_bias):
    consts = _host_consts()
    wq_sw = _swizzle_w(np.ascontiguousarray(np.asarray(L0_attn_Wq).T))
    wk_sw = _swizzle_w(np.ascontiguousarray(np.asarray(L0_attn_Wk).T))
    wv_sw = _swizzle_w(np.ascontiguousarray(np.asarray(L0_attn_Wv).T))
    wf_sw = _swizzle_w(np.ascontiguousarray(np.asarray(L0_ffn_weight_2bit).T)).astype(
        ml_dtypes.bfloat16)
    bq_r = np.ascontiguousarray(np.asarray(L0_attn_bq).reshape(32, 128).T)
    bk_r = np.ascontiguousarray(np.asarray(L0_attn_bk).reshape(32, 128).T)
    bv_r = np.ascontiguousarray(np.asarray(L0_attn_bv).reshape(32, 128).T)
    bias_r = np.ascontiguousarray(np.asarray(L0_ffn_bias).reshape(32, 128).T)
    bw_flat = np.ascontiguousarray(np.asarray(L0_attn_bias_weight).reshape(1, 48))
    wscale = np.ascontiguousarray(np.asarray(L0_ffn_weight_scale).reshape(1, 1))
    x = np.asarray(L0_attn_x)
    in_maps, perms = [], []
    xT = [np.ascontiguousarray(x[b].T) for b in range(2)]
    for c in range(8):
        b, q = c // 4, c % 4
        perm = [2 * q, 2 * q + 1] + [j for j in range(8) if j not in (2 * q, 2 * q + 1)]
        perms.append(perm)
        m = dict(x_sw=_swizzle_x(xT[b], perm), wq_sw=wq_sw, wk_sw=wk_sw,
                 wv_sw=wv_sw, wf_sw=wf_sw, bq_r=bq_r, bk_r=bk_r, bv_r=bv_r,
                 bias_r=bias_r, bw_flat=bw_flat, wscale=wscale, d0g=consts["d0g"],
                 ident_f32=consts["ident_f32"], ident_bf16=consts["ident_bf16"],
                 ones_f32=consts["ones_f32"], ones_bf16=consts["ones_bf16"],
                 invd=consts["invd"], epsc=consts["epsc"], cm15=consts["cm15"])
        in_maps.append(m)
    return in_maps, perms


def kernel(**inputs):
    in_maps, _ = _prep_inputs(**inputs)
    nc = build_program(dbg=False, upto="full")
    res = run_bass_kernel_spmd(nc, in_maps, core_ids=list(range(8)))
    out = np.empty((2, 4096, 4096), np.float32)
    for c in range(8):
        b, q = c // 4, c % 4
        a = np.asarray(res.results[c]["outT"])
        out[b, q * 1024:(q + 1) * 1024, :] = a.transpose(0, 3, 2, 1).reshape(1024, 4096)
    return out


# version 20
# speedup vs baseline: 1.2645x; 1.2645x over previous
import numpy as np
import ml_dtypes
import concourse.bass as bass
import concourse.mybir as mybir
from concourse.bass_utils import run_bass_kernel_spmd

F32 = mybir.dt.float32
BF16 = mybir.dt.bfloat16
I32 = mybir.dt.int32
AF = mybir.ActivationFunctionType
ALU = mybir.AluOpType
AX = mybir.AxisListType

D = 4096
L = 4096
NH = 16
HD = 256
NFB = 32
HALF = 512
SCALE = 1.0 / 16.0
LN_EPS = 1e-5
FFN_EPS = 1e-5
KB = 1024
PARANOID = False

ENGS = ("tensor", "scalar", "vector", "gpsimd", "sync")


def _build_tables():
    g = [1.0, 1.0, 1.0]
    n = 8
    masks = sorted(range(n), key=lambda m: (bin(m).count("1"), m))
    inv = {m: i for i, m in enumerate(masks)}
    C = np.zeros((n, n, n), dtype=np.float32)
    for i, a in enumerate(masks):
        for j, b in enumerate(masks):
            sw, at = 0, a >> 1
            while at:
                sw += bin(at & b).count("1")
                at >>= 1
            s = -1.0 if sw & 1 else 1.0
            for t in range(3):
                if a >> t & 1 and b >> t & 1:
                    s *= float(g[t])
            C[i, j, inv[a ^ b]] = s
    grades = np.array([bin(m).count("1") for m in masks])
    rev = np.where(grades * (grades - 1) // 2 % 2 == 1, -1.0, 1.0).astype(np.float32)
    met = np.array(
        [float(np.prod([g[t] for t in range(3) if m >> t & 1] + [1.0])) for m in masks],
        dtype=np.float32,
    )
    return C, rev, met


def _host_consts():
    C, rev, met = _build_tables()
    biv = C[:, :, [4, 5, 6]]
    d0 = np.zeros((128, 128), np.float32)
    for p in range(128):
        d0[p, p] = rev[p % 8] * met[p % 8] * SCALE
    G = np.zeros((3, 128, 128), np.float32)
    for c in range(3):
        blk = rev[:, None] * biv[:, :, c]
        for dd in range(16):
            G[c, dd * 8:(dd + 1) * 8, dd * 8:(dd + 1) * 8] = blk
    d0g = np.concatenate([d0, G[0], G[1], G[2]], axis=1)
    ident = np.eye(128, dtype=np.float32)
    ones = np.ones((128, 128), np.float32)
    return dict(
        d0g=d0g,
        ident_f32=ident,
        ident_bf16=ident.astype(ml_dtypes.bfloat16),
        ones_f32=ones,
        ones_bf16=ones.astype(ml_dtypes.bfloat16),
        invd=np.full((128, 1), 1.0 / 4096.0, ml_dtypes.bfloat16),
        epsc=np.full((1, 1), LN_EPS, np.float32),
        cm15=np.full((1, 128), -1.5, np.float32),
    )


def _swizzle_w(wT):
    return np.ascontiguousarray(wT.reshape(32, 128, 16, 256).transpose(2, 1, 0, 3))


def _swizzle_x(xT, perm):
    xs = xT.reshape(32, 128, 8, 512).transpose(2, 1, 0, 3)
    return np.ascontiguousarray(xs[perm])


class PSem:
    __slots__ = ("name", "n", "h")

    def __init__(self, name):
        self.name, self.n, self.h = name, 0, None


class Plan:
    def __init__(self):
        self.q = {e: [] for e in ENGS}
        self.wcache = {}

    def emit(self, e, fn):
        self.q[e].append(fn)

    def wait(self, e, sem, val):
        if val <= 0:
            return
        k = (e, sem.name)
        if self.wcache.get(k, -1) >= val:
            return
        self.wcache[k] = val
        self.emit(e, lambda eng, s=sem, v=val: eng.wait_ge(s.h, v))


def build_program(dbg=False, upto="full"):
    nc = bass.Bass("TRN2", target_bir_lowering=False, detect_race_conditions=False)

    x_sw = nc.dram_tensor("x_sw", [8, 128, NFB, 512], F32, kind="ExternalInput").ap()
    wq_sw = nc.dram_tensor("wq_sw", [16, 128, 32, 256], F32, kind="ExternalInput").ap()
    wk_sw = nc.dram_tensor("wk_sw", [16, 128, 32, 256], F32, kind="ExternalInput").ap()
    wv_sw = nc.dram_tensor("wv_sw", [16, 128, 32, 256], F32, kind="ExternalInput").ap()
    wf_sw = nc.dram_tensor("wf_sw", [16, 128, 32, 256], BF16, kind="ExternalInput").ap()
    bq_in = nc.dram_tensor("bq_r", [128, 32], F32, kind="ExternalInput").ap()
    bk_in = nc.dram_tensor("bk_r", [128, 32], F32, kind="ExternalInput").ap()
    bv_in = nc.dram_tensor("bv_r", [128, 32], F32, kind="ExternalInput").ap()
    bb_in = nc.dram_tensor("bias_r", [128, 32], F32, kind="ExternalInput").ap()
    bw_in = nc.dram_tensor("bw_flat", [1, 48], F32, kind="ExternalInput").ap()
    ws_in = nc.dram_tensor("wscale", [1, 1], F32, kind="ExternalInput").ap()
    d0g_in = nc.dram_tensor("d0g", [128, 512], F32, kind="ExternalInput").ap()
    idf_in = nc.dram_tensor("ident_f32", [128, 128], F32, kind="ExternalInput").ap()
    idb_in = nc.dram_tensor("ident_bf16", [128, 128], BF16, kind="ExternalInput").ap()
    onf_in = nc.dram_tensor("ones_f32", [128, 128], F32, kind="ExternalInput").ap()
    onb_in = nc.dram_tensor("ones_bf16", [128, 128], BF16, kind="ExternalInput").ap()
    invd_in = nc.dram_tensor("invd", [128, 1], BF16, kind="ExternalInput").ap()
    eps_in = nc.dram_tensor("epsc", [1, 1], F32, kind="ExternalInput").ap()
    cm15_in = nc.dram_tensor("cm15", [1, 128], F32, kind="ExternalInput").ap()

    outT = nc.dram_tensor("outT", [2, 128, NFB, 512], F32, kind="ExternalOutput").ap()

    xbd = nc.dram_tensor("xbd", [8, 128, NFB, 512], BF16, kind="Internal").ap()
    kd = nc.dram_tensor("kd", [32, 128, L], BF16, kind="Internal").ap()
    vd = nc.dram_tensor("vd", [NH, 8, 128, 4, HD], BF16, kind="Internal").ap()

    dbg_outs = {}
    if dbg:
        def dbgt(name, shape, dt):
            dbg_outs[name] = nc.dram_tensor(name, shape, dt, kind="ExternalOutput").ap()
        dbgt("dbg_xbd", [8, 128, NFB, 512], BF16)
        dbgt("dbg_k", [32, 128, L], BF16)
        dbgt("dbg_v", [NH, 8, 128, 4, HD], BF16)
        dbgt("dbg_qeff", [2, 128, 32 * 512], BF16)
        dbgt("dbg_attn", [2, 128, NFB * 512], F32)
        dbgt("dbg_ln1", [2, 128, NFB * 512], F32)
        dbgt("dbg_ffn", [2, 128, NFB * 512], F32)
        dbgt("dbg_bd", [128, 16 * 128], BF16)

    BASE0 = (nc._sbuf_addr_for_side("left") + 63) // 64 * 64

    def at(name, shape, dtype, off):
        return nc.alloc_sbuf_tensor_at(name, shape, dtype, offset=BASE0 + off).ap()

    o = [0]

    def nxt(sz):
        r = o[0]
        o[0] += (sz + 31) // 32 * 32
        return r

    ident_f32 = at("c_idf", [128, 128], F32, nxt(512))
    ident_bf16 = at("c_idb", [128, 128], BF16, nxt(256))
    ones_f32 = at("c_onf", [128, 128], F32, nxt(512))
    ones_bf16 = at("c_onb", [128, 128], BF16, nxt(256))
    d0g_sb = at("c_d0g", [128, 512], F32, nxt(2048))
    bq_sb = at("c_bq", [128, 32], F32, nxt(128))
    bk_sb = at("c_bk", [128, 32], F32, nxt(128))
    bv_sb = at("c_bv", [128, 32], F32, nxt(128))
    bb_sb = at("c_bias", [128, 32], F32, nxt(128))
    bw_sb = at("c_bw", [1, 48], F32, nxt(192))
    ws_sb = at("c_ws", [1, 1], F32, nxt(32))
    bwb_sb = at("c_bwb", [128, 48], F32, nxt(192))
    bd_sb = at("c_bd", [128, 16 * 128], BF16, nxt(4096))
    invd_sb = at("c_invd", [128, 1], BF16, nxt(32))
    eps_sb = at("c_eps", [1, 1], F32, nxt(32))
    cm15_sb = at("c_cm15", [1, 128], F32, nxt(512))
    stat_sb = at("c_stat", [128, 96], F32, nxt(384))
    am_sb = at("c_am", [128, 8], F32, nxt(32))
    off_rows = nxt(6144)
    rows_sb = at("c_rows", [1, 1536], F32, off_rows)
    off_bca = nxt(2048)
    bca_sb = at("c_bca", [128, 512], F32, off_bca)
    bcb_sb = at("c_bcb", [128, 512], F32, nxt(2048))
    off_tst = nxt(2048)
    tst_sb = at("c_tst", [128, 512], F32, off_tst)
    diag_sb = [at("c_diag0", [128, 8 * 128], BF16, nxt(2048)),
               at("c_diag1", [128, 8 * 128], BF16, off_tst)]
    PB = (o[0] + 63) // 64 * 64
    qstg_sb = [at("c_qstg0", [128, 512], BF16, PB + 160 * KB),
               at("c_qstg1", [128, 512], BF16, PB + 161 * KB)]
    tst2_sb = at("f_tst2", [128, 512], F32, PB + 162 * KB)
    rowB1_sb = at("f_rb1", [1, 512], F32, PB + 164 * KB)
    rowB2_sb = at("f_rb2", [1, 512], F32, PB + 166 * KB)
    mean_b = at("f_meanb", [128, 512], F32, PB + 168 * KB)
    imx_b = at("f_imxb", [128, 512], F32, PB + 170 * KB)
    amw_b = at("f_amwb", [128, 512], F32, PB + 172 * KB)

    NM, SM, S4c, T4c = 0, 8, 16, 24
    MNEG, TOT, INV = 32, 33, 34

    wkh_sb = [at(f"a_wk{i}", [128, 32 * 256], BF16, PB + i * 16 * KB) for i in range(2)]
    wvh_sb = [at(f"a_wv{i}", [128, 32 * 256], BF16, PB + 32 * KB + i * 16 * KB) for i in range(2)]
    xb_sb = [at(f"a_xb{i}", [128, NFB * 512], BF16, PB + 64 * KB + i * 32 * KB) for i in range(2)]
    kstg_sb = [at(f"a_ks{i}", [128, 2 * 512], BF16, PB + 128 * KB + i * 2 * KB) for i in range(3)]
    vstg_sb = [at(f"a_vs{i}", [128, 4 * 256], BF16, PB + 134 * KB + i * 2 * KB) for i in range(3)]
    qeff_sb = at("b_qeff", [128, 32 * 512], BF16, PB + 0)
    xs_sb = at("b_xs", [128, NFB * 512], F32, PB + 32 * KB)
    kh_sb = [at(f"b_kh{i}", [128, 2 * L], BF16, PB + 96 * KB + i * 16 * KB) for i in range(2)]
    vh_sb = [at(f"b_vh{i}", [128, 32 * HD], BF16, PB + 128 * KB + i * 16 * KB) for i in range(2)]
    p_sb = [at("b_p0", [128, L], BF16, PB + 160 * KB),
            at("b_p1", [128, L], BF16, off_rows)]
    pt_sb = at("b_pt", [128, 32 * 256], BF16, PB + 168 * KB)
    wqh_sb = [at(f"b_wq{i}", [128, 32 * 256], BF16, PB + 128 * KB + i * 16 * KB) for i in range(2)]
    xq_sb = at("b_xq", [128, NFB * 512], BF16, PB + 96 * KB)
    xbf_sb = at("f_xbf", [128, NFB * 512], BF16, PB + 0)
    sq_sb = at("f_sq", [128, NFB * 512], BF16, PB + 96 * KB)
    wfh_sb = [at(f"f_wf{i}", [128, 32 * 256], BF16, PB + 96 * KB + i * 16 * KB) for i in range(2)]
    xn_sb = at("f_xn", [128, NFB * 512], BF16, PB + 128 * KB)
    assert BASE0 + PB + 184 * KB <= nc._sbuf_addr_for_side('right'), (BASE0, PB)

    ps_a0 = nc.alloc_psum_tensor("ps_a0", [128, 512], F32).ap()
    ps_a1 = nc.alloc_psum_tensor("ps_a1", [128, 512], F32).ap()
    ps_b0 = nc.alloc_psum_tensor("ps_b0", [128, 512], F32).ap()
    ps_b1 = nc.alloc_psum_tensor("ps_b1", [128, 512], F32).ap()
    ps_c = nc.alloc_psum_tensor("ps_c", [128, 512], F32).ap()
    ps_d = nc.alloc_psum_tensor("ps_d", [128, 512], F32).ap()
    ps_e = nc.alloc_psum_tensor("ps_e", [128, 512], F32).ap()
    ps_f = nc.alloc_psum_tensor("ps_f", [128, 512], F32).ap()

    sLD, sCD, sST = PSem("sLD"), PSem("sCD"), PSem("sST")
    sPE, sACT, sDVE = PSem("sPE"), PSem("sACT"), PSem("sDVE")
    sPOOL = PSem("sPOOL")
    sSTG = PSem("sSTG")
    bsems = {e: PSem("sB_" + e) for e in ENGS}
    all_sems = [sLD, sCD, sST, sPE, sACT, sDVE, sPOOL, sSTG]

    pl = Plan()

    def barrier():
        targets = {s.name: s.n for s in all_sems}
        for e in ENGS:
            for s in all_sems:
                pl.wait(e, s, targets[s.name])
        for e in ENGS:
            b = bsems[e]
            b.n += 1
            pl.emit(e, lambda eng, s=b: eng.sem_inc(s.h, 1))
        for e in ENGS:
            for e2 in ENGS:
                pl.wait(e, bsems[e2], bsems[e2].n)

    ZONE = ["x"]

    def _pq(e):
        if PARANOID is True:
            return True
        if isinstance(PARANOID, (set, frozenset, list, tuple)):
            return ZONE[0] in PARANOID
        return False

    def _paranoid(e):
        if not _pq(e):
            return
        for s in all_sems:
            pl.wait(e, s, s.n)

    def dma(e, out, in_, sem=None):
        _paranoid(e)
        sem = sem or (sCD if e == "gpsimd" else sLD)
        sem.n += 16
        pl.emit(e, lambda eng, o_=out, i_=in_, s=sem: eng.dma_start(out=o_, in_=i_).then_inc(s.h, 16))
        return sem.n

    def mm(out, lhsT, rhs, start, stop, inc=False):
        _paranoid("tensor")
        if _pq("tensor"):
            inc = True
        if inc:
            sPE.n += 1
            pl.emit("tensor", lambda eng, o_=out, l_=lhsT, r_=rhs, st=start, sp=stop:
                    eng.matmul(o_, l_, r_, start=st, stop=sp).then_inc(sPE.h, 1))
            return sPE.n
        pl.emit("tensor", lambda eng, o_=out, l_=lhsT, r_=rhs, st=start, sp=stop:
                eng.matmul(o_, l_, r_, start=st, stop=sp))

    def ptrans(out, in_, ident, inc=False):
        _paranoid("tensor")
        if _pq("tensor"):
            inc = True
        if inc:
            sPE.n += 1
            pl.emit("tensor", lambda eng, o_=out, i_=in_, d_=ident:
                    eng.transpose(o_, i_, d_).then_inc(sPE.h, 1))
            return sPE.n
        pl.emit("tensor", lambda eng, o_=out, i_=in_, d_=ident: eng.transpose(o_, i_, d_))

    def act(out, in_, func=AF.Copy, bias=0.0, scale=1.0, accum=None, inc=False):
        _paranoid("scalar")
        if _pq("scalar"):
            inc = True
        if func == AF.Copy and not isinstance(bias, float):
            func = AF.Identity

        def f(eng, o_=out, i_=in_, fu=func, b_=bias, sc=scale, ac=accum):
            if ac is None:
                return eng.activation(o_, i_, fu, bias=b_, scale=sc)
            return eng.activation(o_, i_, fu, bias=b_, scale=sc, accum_out=ac)
        if inc:
            sACT.n += 1
            pl.emit("scalar", lambda eng, f_=f: f_(eng).then_inc(sACT.h, 1))
            return sACT.n
        pl.emit("scalar", f)

    def pop(builder, inc=False):
        _paranoid("gpsimd")
        if inc:
            sPOOL.n += 1
            pl.emit("gpsimd", lambda eng, b=builder: b(eng).then_inc(sPOOL.h, 1))
            return sPOOL.n
        pl.emit("gpsimd", builder)

    def vop(builder, inc=False):
        _paranoid("vector")
        if _pq("vector"):
            inc = True
        if inc:
            sDVE.n += 1
            pl.emit("vector", lambda eng, b=builder: b(eng).then_inc(sDVE.h, 1))
            return sDVE.n
        pl.emit("vector", builder)

    for dst, src in ((ident_f32[:, :], idf_in), (ident_bf16[:, :], idb_in),
                     (ones_f32[:, :], onf_in), (ones_bf16[:, :], onb_in),
                     (d0g_sb[:, :], d0g_in), (bq_sb[:, :], bq_in),
                     (bk_sb[:, :], bk_in), (bv_sb[:, :], bv_in),
                     (bb_sb[:, :], bb_in), (bw_sb[:, :], bw_in),
                     (ws_sb[:, :], ws_in), (invd_sb[:, :], invd_in),
                     (eps_sb[:, :], eps_in), (cm15_sb[:, :], cm15_in)):
        dma("sync", dst, src)
    v_consts = sLD.n

    pl.wait("tensor", sLD, v_consts)
    v_bwmm = mm(ps_c[:, 0:48], ones_f32[0:1, 0:128], bw_sb[0:1, 0:48], True, True, inc=True)
    pl.wait("scalar", sPE, v_bwmm)
    v_bwb = act(bwb_sb[:, :], ps_c[:, 0:48], inc=True)

    pl.wait("vector", sACT, v_bwb)
    pl.wait("vector", sLD, v_consts)
    v_bd = 0
    for h in range(16):
        g0, g1, g2 = d0g_sb[:, 128:256], d0g_sb[:, 256:384], d0g_sb[:, 384:512]
        d0 = d0g_sb[:, 0:128]
        acc = tst_sb[:, 0:128]
        s0 = bwb_sb[:, 3 * h:3 * h + 1]
        s1 = bwb_sb[:, 3 * h + 1:3 * h + 2]
        s2 = bwb_sb[:, 3 * h + 2:3 * h + 3]
        out = bd_sb[:, h * 128:(h + 1) * 128]
        vop(lambda e, a=acc, g=g0, s=s0: e.tensor_scalar_mul(a, g, s))
        vop(lambda e, a=acc, g=g1, s=s1: e.scalar_tensor_tensor(a, g, s, a, ALU.mult, ALU.add))
        vop(lambda e, a=acc, g=g2, s=s2: e.scalar_tensor_tensor(a, g, s, a, ALU.mult, ALU.add))
        v_bd = vop(lambda e, a=acc, d_=d0, o_=out: e.tensor_add(o_, a, d_), inc=True)

    if dbg:
        pl.wait("sync", sDVE, v_bd)
        dma("sync", dbg_outs["dbg_bd"], bd_sb[:, :], sST)

    ev_ps = {"c": v_bwb, "d": 0, "e": 0, "f": 0}
    st_kstg = [0, 0, 0]
    st_vstg = [0, 0, 0]
    v_xb = [(sLD, 0), (sLD, 0)]
    xb_st = [0, 0]
    v_xst = [0] * 8
    pe_xb_done = [0, 0]
    v_w = [0, 0]
    pe_w_done = [0, 0]

    def load_xchunk(ci):
        par = ci % 2
        lc = ci % 8
        if ci < 8:
            pl.wait("gpsimd", sPE, pe_xb_done[par])
            pl.wait("gpsimd", sST, xb_st[par])
            v_xb[par] = (sCD, dma("gpsimd", xb_sb[par][:, :],
                                  x_sw[lc].rearrange("p fb l -> p (fb l)")))
            pl.wait("sync", sCD, v_xb[par][1])
            v_xst[lc] = dma("sync", xbd[lc].rearrange("p fb l -> p (fb l)"), xb_sb[par][:, :], sST)
            xb_st[par] = v_xst[lc]
        else:
            pl.wait("sync", sPE, pe_xb_done[par])
            pl.wait("sync", sST, v_xst[lc])
            v_xb[par] = (sLD, dma("sync", xb_sb[par][:, :], xbd[lc].rearrange("p fb l -> p (fb l)")))

    def load_wpair(g):
        wpar = g % 2
        pl.wait("gpsimd", sPE, pe_w_done[wpar])
        dma("gpsimd", wkh_sb[wpar][:, :], wk_sw[g])
        v_w[wpar] = dma("gpsimd", wvh_sb[wpar][:, :], wv_sw[g])

    if upto != "none":
        load_xchunk(0)
        load_xchunk(1)
        load_wpair(0)
        load_wpair(1)
        for g in range(16):
            wpar = g % 2
            for lc in range(8):
                ci = g * 8 + lc
                par = ci % 2
                stg = ci % 3
                pl.wait("tensor", sCD, v_w[wpar])
                pl.wait("tensor", v_xb[par][0], v_xb[par][1])
                for ob in range(2):
                    bank = ("c", "d")[ob]
                    ps = (ps_c, ps_d)[ob]
                    pl.wait("tensor", sACT, ev_ps[bank])
                    for ib in range(32):
                        vv = mm(ps[:, 0:512],
                                wkh_sb[wpar][:, ib * 256 + ob * 128: ib * 256 + ob * 128 + 128],
                                xb_sb[par][:, ib * 512: ib * 512 + 512],
                                ib == 0, ib == 31, inc=(ib == 31))
                    pl.wait("scalar", sPE, vv)
                    pl.wait("scalar", sST, st_kstg[stg])
                    ev_ps[bank] = act(kstg_sb[stg][:, ob * 512:(ob + 1) * 512], ps[:, 0:512],
                                      AF.Copy, bias=bk_sb[:, g * 2 + ob: g * 2 + ob + 1],
                                      inc=True)
                pl.wait("sync", sACT, ev_ps["d"])
                st_kstg[stg] = dma(
                    "sync",
                    kd[g * 2:(g + 1) * 2, :, lc * 512:(lc + 1) * 512].rearrange("b p l -> p b l"),
                    kstg_sb[stg][:, :].rearrange("p (b l) -> p b l", b=2), sST)
                for lt in range(4):
                    bank = ("e", "f")[lt % 2]
                    ps = (ps_e, ps_f)[lt % 2]
                    pl.wait("tensor", sACT, ev_ps[bank])
                    for ib in range(32):
                        vv = mm(ps[:, 0:256],
                                xb_sb[par][:, ib * 512 + lt * 128: ib * 512 + lt * 128 + 128],
                                wvh_sb[wpar][:, ib * 256: ib * 256 + 256],
                                ib == 0, ib == 31, inc=(ib == 31))
                    pl.wait("scalar", sPE, vv)
                    pl.wait("scalar", sSTG, st_vstg[stg])
                    ev_ps[bank] = act(vstg_sb[stg][:, lt * 256:(lt + 1) * 256], ps[:, 0:256],
                                      inc=True)
                pl.wait("gpsimd", sACT, ev_ps[("e", "f")[3 % 2]])
                st_vstg[stg] = dma(
                    "gpsimd",
                    vd[g, lc],
                    vstg_sb[stg][:, :].rearrange("p (lt f) -> p lt f", lt=4), sSTG)
                pe_xb_done[par] = sPE.n
                if ci + 2 < 128:
                    load_xchunk(ci + 2)
            pe_w_done[wpar] = sPE.n
            if g + 2 < 16:
                load_wpair(g + 2)

    if dbg:
        pl.wait("sync", sST, sST.n)
        dma("sync", dbg_outs["dbg_xbd"], xbd, sST)
        dma("sync", dbg_outs["dbg_k"], kd, sST)
        dma("sync", dbg_outs["dbg_v"], vd, sST)

    barrier()
    if upto == "kv":
        dma("gpsimd", outT, xbd[0:2])
        for s in all_sems:
            pl.wait("sync", s, s.n)

    next_q = {}

    def q_prefetch(hf, pe_floor_xq, pe_floor_wq):
        pl.wait("sync", sPE, pe_floor_xq)
        v_xq = dma("sync", xq_sb[:, :], xbd[hf].rearrange("p fb l -> p (fb l)"))
        pl.wait("gpsimd", sPE, pe_floor_wq)
        w0 = dma("gpsimd", wqh_sb[0][:, :], wq_sw[0])
        w1 = dma("gpsimd", wqh_sb[1][:, :], wq_sw[1])
        next_q[hf] = (v_xq, [w0, w1])

    def half_program(hf):
        if hf in next_q:
            v_xq, v_wq = next_q.pop(hf)
        else:
            v_xq = dma("sync", xq_sb[:, :], xbd[hf].rearrange("p fb l -> p (fb l)"))
            v_wq = [0, 0]
        pe_wq_done = [0, 0]
        q_ev = [0, 0]
        qe_ev = {"c": 0, "d": 0}
        bl_mm = [0, 0]
        qe_ready = 0

        def load_wq(g):
            wpar = g % 2
            pl.wait("gpsimd", sPE, pe_wq_done[wpar])
            v_wq[wpar] = dma("gpsimd", wqh_sb[wpar][:, :], wq_sw[g])

        if v_wq[0] == 0:
            load_wq(0)
            load_wq(1)
        for g in range(16):
            wpar = g % 2
            pl.wait("tensor", sCD, v_wq[wpar])
            pl.wait("tensor", sLD, v_xq)
            for ob in range(2):
                obg = g * 2 + ob
                ps = (ps_a0, ps_a1)[ob]
                pl.wait("tensor", sACT, q_ev[ob])
                for ib in range(32):
                    vq = mm(ps[:, :],
                            wqh_sb[wpar][:, ib * 256 + ob * 128: ib * 256 + ob * 128 + 128],
                            xq_sb[:, ib * 512: ib * 512 + 512],
                            ib == 0, ib == 31, inc=(ib == 31))
                pl.wait("scalar", sPE, vq)
                pl.wait("scalar", sPE, bl_mm[ob])
                q_ev[ob] = act(qstg_sb[ob][:, :], ps[:, :], AF.Copy,
                               bias=bq_sb[:, obg:obg + 1], inc=True)
            for ob in range(2):
                obg = g * 2 + ob
                h = obg // 2
                cb = ("c", "d")[ob]
                psb = (ps_c, ps_d)[ob]
                pl.wait("tensor", sACT, q_ev[ob])
                pl.wait("tensor", sACT, qe_ev[cb])
                bl_mm[ob] = mm(psb[:, 0:512], bd_sb[:, h * 128:(h + 1) * 128],
                               qstg_sb[ob][:, :], True, True, inc=True)
                pl.wait("scalar", sPE, bl_mm[ob])
                qe_ev[cb] = act(qeff_sb[:, obg * 512:(obg + 1) * 512], psb[:, 0:512], inc=True)
                qe_ready = qe_ev[cb]
            pe_wq_done[wpar] = sPE.n
            if g + 2 < 16:
                load_wq(g + 2)
        if dbg:
            pl.wait("sync", sACT, qe_ready)
            dma("sync", dbg_outs["dbg_qeff"][hf], qeff_sb[:, :], sST)

        v_xs = dma("sync", xs_sb[:, :], x_sw[hf].rearrange("p fb l -> p (fb l)"))
        barrier()
        if upto == "q":
            return

        v_kh = [0, 0]
        v_vh = [0, 0]
        kh_free = [0, 0]
        s_bufs = (ps_a0, ps_b0, ps_a1, ps_b1)
        s_bank_ev = [0, 0, 0, 0]
        s_rot = [0]
        bank_act = {"c": 0, "d": 0, "e": 0, "f": 0, "g": 0}
        bank_dve = {"c": 0, "d": 0, "e": 0, "f": 0, "g": 0}
        PSB = {"c": ps_c, "d": ps_d, "e": ps_e, "f": ps_f, "g": ps_b1}
        v_pt_last = [0, 0]
        v_av_last = 0
        ev_p_qt = [0, 0]
        st_done = [0, 0]
        av_ready_a = [0, 0]
        av_ready_d = [0, 0]
        av_fin = [0] * NH
        qeff_fin = [0] * NH
        pt4 = pt_sb[:, :].rearrange("p (q mtl) -> p q mtl", q=2)
        v_kh[0] = dma("sync", kh_sb[0][:, :].rearrange("p (b m) -> p b m", b=2),
                      kd[0:2].rearrange("b p m -> p b m"))
        v_vh[0] = dma("sync", vh_sb[0][:, :].rearrange("p (lc x) -> p lc x", lc=8),
                      vd[0].rearrange("lc p lt f -> p lc (lt f)"))
        pending = None

        for h in range(NH):
            par = h % 2

            def S_qt(qt):
                ZONE[0] = "S"
                pq = qt % 2
                sb_off = 48 * pq
                for ck in range(8):
                    sb = s_rot[0] % 4
                    s_rot[0] += 1
                    ps = s_bufs[sb]
                    pl.wait("tensor", sACT, s_bank_ev[sb])
                    if sb == 3:
                        pl.wait("tensor", sACT, bank_act["g"])
                        pl.wait("tensor", sDVE, bank_dve["g"])
                    for fb2 in range(2):
                        vS = mm(ps[:, :],
                                qeff_sb[:, (2 * h + fb2) * 512 + qt * 128:
                                        (2 * h + fb2) * 512 + qt * 128 + 128],
                                kh_sb[par][:, fb2 * L + ck * 512:
                                           fb2 * L + ck * 512 + 512],
                                fb2 == 0, fb2 == 1, inc=(fb2 == 1))
                    pl.wait("vector", sPE, vS)
                    v_nm = vop(lambda e, ps_=ps, c_=ck + sb_off:
                               e.reduce_max(stat_sb[:, NM + c_:NM + c_ + 1], ps_[:, :],
                                            axis=AX.X, negate=True), inc=True)
                    pl.wait("scalar", sDVE, v_nm)
                    pl.wait("scalar", sPE, v_pt_last[pq])
                    s_bank_ev[sb] = act(p_sb[pq][:, ck * 512:(ck + 1) * 512], ps[:, :],
                                        AF.Exp,
                                        bias=stat_sb[:, NM + sb_off + ck:NM + sb_off + ck + 1],
                                        scale=1.0,
                                        accum=stat_sb[:, SM + sb_off + ck:SM + sb_off + ck + 1],
                                        inc=True)
                ev_p_qt[pq] = s_bank_ev[(s_rot[0] - 1) % 4]

            def ST_qt(qt):
                ZONE[0] = "stats"
                pq = qt % 2
                c0 = 48 * pq
                pl.wait("vector", sACT, ev_p_qt[pq])
                pl.wait("vector", sPE, v_pt_last[pq])
                v1 = vop(lambda e: e.tensor_reduce(stat_sb[:, c0 + MNEG:c0 + MNEG + 1],
                                                   stat_sb[:, c0 + NM:c0 + NM + 8],
                                                   AX.X, ALU.min), inc=True)
                pl.wait("scalar", sDVE, v1)
                v3 = act(stat_sb[:, c0 + S4c:c0 + S4c + 8], stat_sb[:, c0 + NM:c0 + NM + 8],
                         AF.Exp, bias=stat_sb[:, c0 + MNEG:c0 + MNEG + 1], scale=-1.0, inc=True)
                pl.wait("vector", sACT, v3)
                v4 = vop(lambda e: e.scalar_tensor_tensor(
                    stat_sb[:, c0 + T4c:c0 + T4c + 8], stat_sb[:, c0 + S4c:c0 + S4c + 8],
                    1.0, stat_sb[:, c0 + SM:c0 + SM + 8], ALU.mult, ALU.mult,
                    accum_out=stat_sb[:, c0 + TOT:c0 + TOT + 1]), inc=True)
                pl.wait("vector", sDVE, v4)
                v5 = vop(lambda e: e.reciprocal(stat_sb[:, c0 + INV:c0 + INV + 1],
                                                stat_sb[:, c0 + TOT:c0 + TOT + 1]), inc=True)
                pl.wait("vector", sDVE, v5)
                vd_ = 0
                for ck in range(8):
                    vd_ = vop(lambda e, c_=ck:
                              e.tensor_scalar(diag_sb[pq][:, c_ * 128:(c_ + 1) * 128],
                                              ident_bf16[:, :],
                                              stat_sb[:, c0 + S4c + c_:c0 + S4c + c_ + 1],
                                              stat_sb[:, c0 + INV:c0 + INV + 1],
                                              ALU.mult, ALU.mult), inc=True)
                st_done[pq] = vd_

            def PT_qt(qt):
                ZONE[0] = "PT"
                pq = qt % 2
                qtl = qt % 2
                pl.wait("tensor", sDVE, st_done[pq])
                pl.wait("tensor", sACT, ev_p_qt[pq])
                last_a, last_d = 0, 0
                for r in range(8):
                    bk = ("c", "d", "f")[r % 3]
                    ps = PSB[bk]
                    pl.wait("tensor", sACT, bank_act[bk])
                    pl.wait("tensor", sDVE, bank_dve[bk])
                    for j in range(4):
                        mt = 4 * r + j
                        vPT = mm(ps[:, j * 128:(j + 1) * 128],
                                 p_sb[pq][:, mt * 128:(mt + 1) * 128],
                                 diag_sb[pq][:, (mt // 4) * 128:(mt // 4) * 128 + 128],
                                 True, True, inc=(j == 3))
                    v_pt_last[pq] = vPT
                    dst = pt_sb[:, qtl * 4096 + r * 512: qtl * 4096 + (r + 1) * 512]
                    if r % 2 == 0:
                        pl.wait("scalar", sPE, vPT)
                        pl.wait("scalar", sPE, v_av_last)
                        last_a = act(dst, ps[:, :], inc=True)
                        bank_act[bk] = last_a
                    else:
                        pl.wait("vector", sPE, vPT)
                        pl.wait("vector", sPE, v_av_last)
                        last_d = vop(lambda e, d_=dst, ps_=ps: e.tensor_copy(d_, ps_[:, :]),
                                     inc=True)
                        bank_dve[bk] = last_d
                av_ready_a[qt // 2] = last_a
                av_ready_d[qt // 2] = last_d

            def AV_qp(qp, hh, pp):
                nonlocal v_av_last
                ZONE[0] = "AV"
                pl.wait("tensor", sACT, av_ready_a[qp])
                pl.wait("tensor", sDVE, av_ready_d[qp])
                pl.wait("tensor", sLD, v_vh[pp])
                pl.wait("tensor", sDVE, bank_dve["e"])
                for fb2 in range(2):
                    for mt in range(32):
                        vAV = mm(ps_e[:, fb2 * 256:(fb2 + 1) * 256],
                                 vh_sb[pp][:, mt * HD + fb2 * 128: mt * HD + fb2 * 128 + 128],
                                 pt4[:, :, mt * 128:(mt + 1) * 128],
                                 mt == 0, mt == 31, inc=(mt == 31))
                v_av_last = vAV
                pl.wait("vector", sPE, vAV)
                pl.wait("vector", sLD, v_xs)
                for fb2 in range(2):
                    fb = 2 * hh + fb2
                    xsl = xs_sb[:, fb * 512 + qp * 256: fb * 512 + qp * 256 + 256]
                    bank_dve["e"] = vop(
                        lambda e, s_=ps_e[:, fb2 * 256:(fb2 + 1) * 256], x_=xsl,
                        b_=bv_sb[:, fb:fb + 1]:
                        e.scalar_tensor_tensor(x_, s_, b_, x_, ALU.add, ALU.add),
                        inc=True)
                if qp == 1:
                    av_fin[hh] = bank_dve["e"]

            pl.wait("tensor", sLD, v_kh[par])
            S_qt(0)
            ST_qt(0)
            S_qt(1)
            if pending is not None:
                AV_qp(*pending)
                pending = None
            if h + 1 < NH:
                np_ = (h + 1) % 2
                kh_free[np_] = sPE.n
                pl.wait("sync", sPE, kh_free[np_])
                v_kh[np_] = dma("sync", kh_sb[np_][:, :].rearrange("p (b m) -> p b m", b=2),
                                kd[2 * (h + 1):2 * (h + 1) + 2].rearrange("b p m -> p b m"))
                v_vh[np_] = dma("sync", vh_sb[np_][:, :].rearrange("p (lc x) -> p lc x", lc=8),
                                vd[h + 1].rearrange("lc p lt f -> p lc (lt f)"))
            ST_qt(1)
            PT_qt(0)
            PT_qt(1)
            S_qt(2)
            ST_qt(2)
            S_qt(3)
            qeff_fin[h] = sPE.n
            AV_qp(0, h, par)
            ST_qt(3)
            PT_qt(2)
            PT_qt(3)
            pending = (1, h, par)
            ZONE[0] = "x"
        AV_qp(*pending)
        pending = None
        attn_pe_end = sPE.n
        attn_marks = [(sDVE, bank_dve["e"]), (sACT, bank_act["c"]), (sACT, bank_act["d"]),
                      (sACT, bank_act["f"]), (sDVE, bank_dve["c"]), (sDVE, bank_dve["d"]),
                      (sDVE, bank_dve["f"]), (sACT, s_bank_ev[0]), (sACT, s_bank_ev[1]),
                      (sACT, s_bank_ev[2]), (sACT, s_bank_ev[3]),
                      (sACT, bank_act["g"]), (sDVE, bank_dve["g"])]
        if dbg:
            pl.wait("sync", sDVE, sDVE.n)
            dma("sync", dbg_outs["dbg_attn"][hf], xs_sb[:, :], sST)
            barrier()
        if upto == "attn":
            pl.wait("sync", sDVE, sDVE.n)
            dma("sync", outT[hf], xs_sb[:, :].rearrange("p (fb l) -> p fb l", fb=NFB), sST)
            barrier()
            return

        def layer_norm(dump_key, cast_wait=None, sq_wait=None, bc_wait=None,
                       psum_waits=()):
            vcast, vsqv = [None] * NFB, [0] * NFB
            for fb in range(NFB):
                cp = (lambda e, f_=fb: e.tensor_copy(
                    xbf_sb[:, f_ * 512:(f_ + 1) * 512], xs_sb[:, f_ * 512:(f_ + 1) * 512]))
                if fb % 2 == 0:
                    if cast_wait is not None:
                        cast_wait(fb, "gpsimd")
                    vcast[fb] = (sPOOL, pop(cp, inc=True))
                else:
                    if cast_wait is not None:
                        cast_wait(fb, "vector")
                    vcast[fb] = (sDVE, vop(cp, inc=True))
            if sq_wait is not None:
                sq_wait()
            for fb in range(NFB):
                pl.wait("vector", *vcast[fb])
                vsqv[fb] = vop(lambda e, f_=fb: e.tensor_mul(
                    sq_sb[:, f_ * 512:(f_ + 1) * 512], xbf_sb[:, f_ * 512:(f_ + 1) * 512],
                    xbf_sb[:, f_ * 512:(f_ + 1) * 512]), inc=True)
            for sm_, val_ in psum_waits:
                pl.wait("tensor", sm_, val_)
            for fb in range(NFB):
                pl.wait("tensor", *vcast[fb])
                mm(ps_e[0:1, 0:512], invd_sb[:, 0:1],
                   xbf_sb[:, fb * 512:(fb + 1) * 512], fb == 0, fb == NFB - 1,
                   inc=(fb == NFB - 1))
                pl.wait("tensor", sDVE, vsqv[fb])
                vsq = mm(ps_f[0:1, 0:512], invd_sb[:, 0:1],
                         sq_sb[:, fb * 512:(fb + 1) * 512], fb == 0, fb == NFB - 1,
                         inc=(fb == NFB - 1))
            pe_sums = vsq
            pl.wait("scalar", sPE, vsq)
            act(rows_sb[0:1, 0:512], ps_e[0:1, 0:512])
            v_rows = act(rows_sb[0:1, 512:1024], ps_f[0:1, 0:512], inc=True)
            pl.wait("vector", sACT, v_rows)
            v_m2 = vop(lambda e: e.tensor_mul(rows_sb[0:1, 1024:1536], rows_sb[0:1, 0:512],
                                              rows_sb[0:1, 0:512]), inc=True)
            pl.wait("vector", sDVE, v_m2)
            v_var = vop(lambda e: e.scalar_tensor_tensor(
                rows_sb[0:1, 512:1024], rows_sb[0:1, 1024:1536], -1.0,
                rows_sb[0:1, 512:1024], ALU.mult, ALU.add), inc=True)
            pl.wait("scalar", sDVE, v_var)
            act(rows_sb[0:1, 512:1024], rows_sb[0:1, 512:1024], AF.Ln,
                bias=eps_sb[0:1, 0:1], scale=1.0)
            v_rstd = act(rows_sb[0:1, 512:1024], rows_sb[0:1, 512:1024], AF.Exp,
                         bias=0.0, scale=-0.5, inc=True)
            pl.wait("vector", sACT, v_rstd)
            v_mr = vop(lambda e: e.scalar_tensor_tensor(
                rows_sb[0:1, 0:512], rows_sb[0:1, 0:512], -1.0,
                rows_sb[0:1, 512:1024], ALU.mult, ALU.mult), inc=True)
            pl.wait("tensor", sDVE, v_mr)
            mm(ps_c[:, 0:512], ones_f32[0:1, 0:128], rows_sb[0:1, 512:1024], True, True, inc=True)
            v_b2 = mm(ps_d[:, 0:512], ones_f32[0:1, 0:128], rows_sb[0:1, 0:512], True, True, inc=True)
            pl.wait("scalar", sPE, v_b2)
            if bc_wait is not None:
                bc_wait()
            act(bca_sb[:, :], ps_c[:, 0:512])
            v_bc = act(bcb_sb[:, :], ps_d[:, 0:512], inc=True)
            pl.wait("vector", sACT, v_bc)
            pl.wait("gpsimd", sACT, v_bc)
            nfin = [None] * NFB
            v_ln_d = v_ln_p = 0
            for fb in range(NFB):
                sl = xs_sb[:, fb * 512:(fb + 1) * 512]
                if fb < 20:
                    vop(lambda e, s_=sl: e.tensor_mul(s_, s_, bca_sb[:, :]))
                    v_ln_d = vop(lambda e, s_=sl: e.tensor_add(s_, s_, bcb_sb[:, :]), inc=True)
                    nfin[fb] = (sDVE, v_ln_d)
                else:
                    pop(lambda e, s_=sl: e.tensor_mul(s_, s_, bca_sb[:, :]))
                    v_ln_p = pop(lambda e, s_=sl: e.tensor_add(s_, s_, bcb_sb[:, :]), inc=True)
                    nfin[fb] = (sPOOL, v_ln_p)
            if dbg and dump_key is not None:
                pl.wait("sync", sDVE, v_ln_d)
                pl.wait("sync", sPOOL, v_ln_p)
                dma("sync", dbg_outs[dump_key][hf], xs_sb[:, :], sST)
            return dict(nfin=nfin, pe_sums=pe_sums, d=v_ln_d, p=v_ln_p)

        vcast, vsqv = [0] * NFB, [0] * NFB
        for fb in range(NFB):
            pl.wait("gpsimd", sDVE, av_fin[fb // 2])
            pl.wait("gpsimd", sPE, qeff_fin[fb // 2])
            vcast[fb] = pop(lambda e, f_=fb: e.tensor_copy(
                xbf_sb[:, f_ * 512:(f_ + 1) * 512], xs_sb[:, f_ * 512:(f_ + 1) * 512]),
                inc=True)
        pl.wait("vector", sPE, qeff_fin[NH - 1])
        for fb in range(NFB):
            pl.wait("vector", sPOOL, vcast[fb])
            vsqv[fb] = vop(lambda e, f_=fb: e.tensor_mul(
                sq_sb[:, f_ * 512:(f_ + 1) * 512], xbf_sb[:, f_ * 512:(f_ + 1) * 512],
                xbf_sb[:, f_ * 512:(f_ + 1) * 512]), inc=True)
        for sm_, val_ in attn_marks:
            pl.wait("tensor", sm_, val_)
        for fb in range(NFB):
            pl.wait("tensor", sPOOL, vcast[fb])
            mm(ps_e[0:1, 0:512], invd_sb[:, 0:1],
               xbf_sb[:, fb * 512:(fb + 1) * 512], fb == 0, fb == NFB - 1,
               inc=(fb == NFB - 1))
            pl.wait("tensor", sDVE, vsqv[fb])
            vsq = mm(ps_f[0:1, 0:512], invd_sb[:, 0:1],
                     sq_sb[:, fb * 512:(fb + 1) * 512], fb == 0, fb == NFB - 1,
                     inc=(fb == NFB - 1))
        ln1_pe_sums = vsq
        v_wf = [0, 0]
        pe_wf_done = [0, 0]

        def load_wf(g):
            wpar = g % 2
            gate = max(pe_wf_done[wpar], ln1_pe_sums)
            pl.wait("sync", sPE, gate)
            dma("sync", wfh_sb[wpar][:, 0:4096], wf_sw[g, :, 0:16, :], sLD)
            pl.wait("scalar", sPE, gate)
            v_wf[wpar] = dma("scalar", wfh_sb[wpar][:, 4096:8192], wf_sw[g, :, 16:32, :],
                             sLD)

        load_wf(0)
        load_wf(1)
        tr_dve = [0, 0]
        trg_fin = [0] * 8
        v_am = 0
        for lt in range(4):
            for r in range(8):
                sbk = r % 2
                ps = (ps_a0, ps_a1)[sbk]
                pl.wait("tensor", sDVE, tr_dve[sbk])
                for j in range(4):
                    fb = r * 4 + j
                    pl.wait("tensor", sDVE, av_fin[fb // 2])
                    vT = ptrans(ps[:, j * 128:(j + 1) * 128],
                                xs_sb[:, fb * 512 + lt * 128: fb * 512 + lt * 128 + 128],
                                ident_f32[:, :], inc=(j == 3))
                if lt == 3:
                    trg_fin[r] = vT
                pl.wait("vector", sPE, vT)
                vop(lambda e, ps_=ps, r_=r:
                    e.reduce_max(stat_sb[:, r_:r_ + 1], ps_[:, :], axis=AX.X))
                tr_dve[sbk] = vop(lambda e, ps_=ps, r_=r:
                                  e.tensor_reduce(stat_sb[:, 8 + r_:9 + r_], ps_[:, :],
                                                  AX.X, ALU.min), inc=True)
            pl.wait("vector", sDVE, tr_dve[0])
            pl.wait("vector", sDVE, tr_dve[1])
            vop(lambda e, l_=lt: e.tensor_reduce(am_sb[:, l_:l_ + 1],
                                                 stat_sb[:, 0:8], AX.X, ALU.max))
            v_am = vop(lambda e, l_=lt: e.tensor_reduce(am_sb[:, 4 + l_:5 + l_],
                                                        stat_sb[:, 8:16], AX.X, ALU.min),
                       inc=True)
        pl.wait("scalar", sPE, ln1_pe_sums)
        act(rows_sb[0:1, 0:512], ps_e[0:1, 0:512])
        v_rows = act(rows_sb[0:1, 512:1024], ps_f[0:1, 0:512], inc=True)
        pl.wait("tensor", sACT, v_rows)
        v_mb = mm(ps_c[:, 0:512], ones_f32[0:1, 0:128], rows_sb[0:1, 0:512], True, True,
                  inc=True)
        pl.wait("scalar", sPE, v_mb)
        v_mbe = act(mean_b[:, :], ps_c[:, 0:512], inc=True)
        pl.wait("tensor", sDVE, v_am)
        for lt in range(4):
            ptrans(ps_e[0:1, lt * 128:(lt + 1) * 128], am_sb[:, lt:lt + 1], ident_f32[:, :])
            vT2 = ptrans(ps_f[0:1, lt * 128:(lt + 1) * 128], am_sb[:, 4 + lt:5 + lt],
                         ident_f32[:, :], inc=(lt == 3))
        pl.wait("scalar", sPE, vT2)
        act(rowB1_sb[0:1, :], ps_e[0:1, 0:512])
        v_rB = act(rowB2_sb[0:1, :], ps_f[0:1, 0:512], inc=True)
        pl.wait("vector", sACT, v_rB)
        vop(lambda e: e.tensor_sub(rowB1_sb[0:1, :], rowB1_sb[0:1, :], rows_sb[0:1, 0:512]))
        vop(lambda e: e.tensor_sub(rowB2_sb[0:1, :], rows_sb[0:1, 0:512], rowB2_sb[0:1, :]))
        vop(lambda e: e.tensor_max(rowB2_sb[0:1, :], rowB2_sb[0:1, :], rowB1_sb[0:1, :]))
        v_mx = vop(lambda e: e.tensor_scalar_max(rowB2_sb[0:1, :], rowB2_sb[0:1, :], 1e-30),
                   inc=True)
        pl.wait("vector", sDVE, v_mx)
        v_imx = vop(lambda e: e.reciprocal(rowB1_sb[0:1, :], rowB2_sb[0:1, :]), inc=True)
        pl.wait("tensor", sDVE, v_imx)
        v_ib = mm(ps_d[:, 0:512], ones_f32[0:1, 0:128], rowB1_sb[0:1, :], True, True,
                  inc=True)
        pl.wait("scalar", sPE, v_ib)
        v_ibe = act(imx_b[:, :], ps_d[:, 0:512], inc=True)
        pl.wait("vector", sACT, v_ibe)
        pl.wait("gpsimd", sACT, v_ibe)
        pl.wait("vector", sPOOL, vcast[NFB - 1])
        vxn_d, vxn_p = [0] * NFB, [0] * NFB
        for fb in range(NFB):
            sl = xs_sb[:, fb * 512:(fb + 1) * 512]
            xn = xn_sb[:, fb * 512:(fb + 1) * 512]
            if fb % 8 < 6:
                vop(lambda e, x_=xn, s_=sl: e.tensor_sub(x_, s_, mean_b[:, :]))
                vxn_d[fb] = vop(lambda e, x_=xn: e.tensor_mul(x_, x_, imx_b[:, :]),
                                inc=True)
            else:
                pop(lambda e, x_=xn, s_=sl: e.tensor_sub(x_, s_, mean_b[:, :]))
                vxn_p[fb] = pop(lambda e, x_=xn: e.tensor_mul(x_, x_, imx_b[:, :]),
                                inc=True)
        for fb in range(NFB):
            pl.wait("tensor", sDVE, vxn_d[fb])
            pl.wait("tensor", sPOOL, vxn_p[fb])
            vxs = mm(ps_e[0:1, 0:512], ones_bf16[:, 0:1],
                     xn_sb[:, fb * 512:(fb + 1) * 512], fb == 0, fb == NFB - 1,
                     inc=(fb == NFB - 1))
        pl.wait("scalar", sPE, vxs)
        v_xnsum = act(rows_sb[0:1, 1024:1536], ps_e[0:1, 0:512], inc=True)
        pl.wait("tensor", sACT, v_xnsum)
        pl.wait("vector", sACT, v_rows)
        v_m2 = vop(lambda e: e.tensor_mul(tst_sb[0:1, 0:512], rows_sb[0:1, 0:512],
                                          rows_sb[0:1, 0:512]), inc=True)
        pl.wait("vector", sDVE, v_m2)
        v_var = vop(lambda e: e.scalar_tensor_tensor(
            rows_sb[0:1, 512:1024], tst_sb[0:1, 0:512], -1.0,
            rows_sb[0:1, 512:1024], ALU.mult, ALU.add), inc=True)
        pl.wait("scalar", sDVE, v_var)
        act(rows_sb[0:1, 512:1024], rows_sb[0:1, 512:1024], AF.Ln,
            bias=eps_sb[0:1, 0:1], scale=1.0)
        v_rstd = act(rows_sb[0:1, 512:1024], rows_sb[0:1, 512:1024], AF.Exp,
                     bias=0.0, scale=-0.5, inc=True)
        pl.wait("vector", sACT, v_rstd)
        vop(lambda e: e.tensor_mul(rowB2_sb[0:1, :], rowB2_sb[0:1, :],
                                   rows_sb[0:1, 512:1024]))
        vop(lambda e: e.tensor_scalar_max(rowB2_sb[0:1, :], rowB2_sb[0:1, :], FFN_EPS))
        v_amw = vop(lambda e: e.tensor_scalar_mul(rowB2_sb[0:1, :], rowB2_sb[0:1, :],
                                                  ws_sb[0:1, 0:1]), inc=True)
        v_mr = vop(lambda e: e.scalar_tensor_tensor(
            rows_sb[0:1, 0:512], rows_sb[0:1, 0:512], -1.0,
            rows_sb[0:1, 512:1024], ALU.mult, ALU.mult), inc=True)
        pl.wait("tensor", sDVE, v_mr)
        pl.wait("tensor", sDVE, v_amw)
        mm(ps_e[:, 0:512], ones_f32[0:1, 0:128], rows_sb[0:1, 512:1024], True, True, inc=True)
        mm(ps_f[:, 0:512], ones_f32[0:1, 0:128], rows_sb[0:1, 0:512], True, True, inc=True)
        v_b3 = mm(ps_b0[:, 0:512], ones_f32[0:1, 0:128], rowB2_sb[0:1, :], True, True, inc=True)
        pl.wait("scalar", sPE, v_b3)
        act(bca_sb[:, :], ps_e[:, 0:512])
        act(bcb_sb[:, :], ps_f[:, 0:512])
        v_bc = act(amw_b[:, :], ps_b0[:, 0:512], inc=True)
        pl.wait("vector", sACT, v_bc)
        pl.wait("gpsimd", sACT, v_bc)
        nfin = [None] * NFB
        v_ln_d = v_ln_p = 0
        for fb in range(NFB):
            sl = xs_sb[:, fb * 512:(fb + 1) * 512]
            pl.wait("vector" if fb < 20 else "gpsimd", sPE, trg_fin[fb // 4])
            if fb < 20:
                pl.wait("vector", sDVE, vxn_d[fb])
                pl.wait("vector", sPOOL, vxn_p[fb])
                vop(lambda e, s_=sl: e.tensor_mul(s_, s_, bca_sb[:, :]))
                v_ln_d = vop(lambda e, s_=sl: e.tensor_add(s_, s_, bcb_sb[:, :]), inc=True)
                nfin[fb] = (sDVE, v_ln_d)
            else:
                pl.wait("gpsimd", sDVE, vxn_d[fb])
                pop(lambda e, s_=sl: e.tensor_mul(s_, s_, bca_sb[:, :]))
                v_ln_p = pop(lambda e, s_=sl: e.tensor_add(s_, s_, bcb_sb[:, :]), inc=True)
                nfin[fb] = (sPOOL, v_ln_p)
        ln1 = dict(nfin=nfin, pe_sums=ln1_pe_sums, d=v_ln_d, p=v_ln_p)
        if dbg:
            pl.wait("sync", sDVE, v_ln_d)
            pl.wait("sync", sPOOL, v_ln_p)
            dma("sync", dbg_outs["dbg_ln1"][hf], xs_sb[:, :], sST)
        if upto == "ln1":
            pl.wait("sync", sDVE, ln1["d"])
            pl.wait("sync", sPOOL, ln1["p"])
            dma("sync", outT[hf], xs_sb[:, :].rearrange("p (fb l) -> p fb l", fb=NFB), sST)
            barrier()
            return

        ffn_fin = [0] * NFB
        bank_free = {"c": 0, "d": 0}
        tst_free = [("d", 0), ("d", 0)]
        v_ffn_d = 0
        pl.wait("tensor", sACT, v_ibe)
        for g in range(16):
            wpar = g % 2
            pl.wait("tensor", sLD, v_wf[wpar])
            for ob in range(2):
                obg = g * 2 + ob
                cb = ("c", "d")[ob]
                ps = (ps_c, ps_d)[ob]
                pl.wait("tensor", sDVE, bank_free[cb])
                for ib in range(32):
                    mm(ps[:, 0:512],
                       wfh_sb[wpar][:, ib * 256 + ob * 128: ib * 256 + ob * 128 + 128],
                       xn_sb[:, ib * 512: ib * 512 + 512],
                       ib == 0, False, inc=False)
                vF = mm(ps[:, 0:512], cm15_sb[0:1, 0:128], rows_sb[0:1, 1024:1536],
                        False, True, inc=True)
                pl.wait("vector", sPE, vF)
                eng_prev, val_prev = tst_free[ob]
                pl.wait("vector", sDVE if eng_prev == "d" else sPOOL, val_prev)
                tp = (tst_sb, tst2_sb)[ob]
                vmul = vop(lambda e, ps_=ps, t_=tp: e.tensor_mul(t_[:, :], ps_[:, 0:512],
                                                                 amw_b[:, :]), inc=True)
                bank_free[cb] = vmul
                xsl = xs_sb[:, obg * 512:(obg + 1) * 512]
                sm_, val_ = nfin[obg]
                pl.wait("vector", sm_, val_)
                v_ffn_d = vop(
                    lambda e, x_=xsl, t_=tp, b_=bb_sb[:, obg:obg + 1]:
                    e.scalar_tensor_tensor(x_, t_[:, :], b_, x_, ALU.add, ALU.add),
                    inc=True)
                ffn_fin[obg] = v_ffn_d
                tst_free[ob] = ("d", v_ffn_d)
            pe_wf_done[wpar] = sPE.n
            if g + 2 < 16:
                load_wf(g + 2)
        ffn_pe_end = sPE.n
        ffn_dve_end = sDVE.n
        if dbg:
            pl.wait("sync", sDVE, sDVE.n)
            dma("sync", dbg_outs["dbg_ffn"][hf], xs_sb[:, :], sST)
            barrier()
        if upto == "ffn":
            pl.wait("sync", sDVE, sDVE.n)
            dma("sync", outT[hf], xs_sb[:, :].rearrange("p (fb l) -> p fb l", fb=NFB), sST)
            barrier()
            return

        ln2 = layer_norm(
            None,
            cast_wait=lambda fb, e: pl.wait(e, sDVE, ffn_fin[fb]),
            sq_wait=lambda: pl.wait("vector", sPE, ffn_pe_end),
            bc_wait=lambda: (pl.wait("scalar", sDVE, ffn_dve_end)),
            psum_waits=[(sACT, v_xnsum), (sACT, v_bc),
                        (sDVE, bank_free["c"]), (sDVE, bank_free["d"])])
        for gq in range(4):
            for fb in range(gq * 8, gq * 8 + 8):
                sm, val = ln2["nfin"][fb]
                pl.wait("sync", sm, val)
            dma("sync", outT[hf, :, gq * 8:(gq + 1) * 8, :],
                xs_sb[:, gq * 8 * 512:(gq + 1) * 8 * 512].rearrange(
                    "p (fb l) -> p fb l", fb=8), sST)
        if hf == 0:
            q_prefetch(1, ln2["pe_sums"], ffn_pe_end)
        barrier()

    if upto != "kv":
        for hf in range(2):
            half_program(hf)

    for s in all_sems:
        pl.wait("sync", s, s.n)

    import contextlib
    st = contextlib.ExitStack()
    for s in all_sems + list(bsems.values()):
        s.h = st.enter_context(nc.semaphore(s.name))
    with st:
        with nc.Block() as blk:
            blk.tensor(lambda eng: [f(eng) for f in pl.q["tensor"]])
            blk.scalar(lambda eng: [f(eng) for f in pl.q["scalar"]])
            blk.vector(lambda eng: [f(eng) for f in pl.q["vector"]])
            blk.gpsimd(lambda eng: [f(eng) for f in pl.q["gpsimd"]])
            blk.sync(lambda eng: [f(eng) for f in pl.q["sync"]])
    return nc


def _prep_inputs(L0_attn_x, L0_attn_Wq, L0_attn_bq, L0_attn_Wk, L0_attn_bk,
                 L0_attn_Wv, L0_attn_bv, L0_attn_bias_weight,
                 L0_ffn_weight_2bit, L0_ffn_weight_scale, L0_ffn_bias):
    consts = _host_consts()
    wq_sw = _swizzle_w(np.ascontiguousarray(np.asarray(L0_attn_Wq).T))
    wk_sw = _swizzle_w(np.ascontiguousarray(np.asarray(L0_attn_Wk).T))
    wv_sw = _swizzle_w(np.ascontiguousarray(np.asarray(L0_attn_Wv).T))
    wf_sw = _swizzle_w(np.ascontiguousarray(np.asarray(L0_ffn_weight_2bit).T)).astype(
        ml_dtypes.bfloat16)
    bq_r = np.ascontiguousarray(np.asarray(L0_attn_bq).reshape(32, 128).T)
    bk_r = np.ascontiguousarray(np.asarray(L0_attn_bk).reshape(32, 128).T)
    bv_r = np.ascontiguousarray(np.asarray(L0_attn_bv).reshape(32, 128).T)
    bias_r = np.ascontiguousarray(np.asarray(L0_ffn_bias).reshape(32, 128).T)
    bw_flat = np.ascontiguousarray(np.asarray(L0_attn_bias_weight).reshape(1, 48))
    wscale = np.ascontiguousarray(np.asarray(L0_ffn_weight_scale).reshape(1, 1))
    x = np.asarray(L0_attn_x)
    in_maps, perms = [], []
    xT = [np.ascontiguousarray(x[b].T) for b in range(2)]
    for c in range(8):
        b, q = c // 4, c % 4
        perm = [2 * q, 2 * q + 1] + [j for j in range(8) if j not in (2 * q, 2 * q + 1)]
        perms.append(perm)
        m = dict(x_sw=_swizzle_x(xT[b], perm), wq_sw=wq_sw, wk_sw=wk_sw,
                 wv_sw=wv_sw, wf_sw=wf_sw, bq_r=bq_r, bk_r=bk_r, bv_r=bv_r,
                 bias_r=bias_r, bw_flat=bw_flat, wscale=wscale, d0g=consts["d0g"],
                 ident_f32=consts["ident_f32"], ident_bf16=consts["ident_bf16"],
                 ones_f32=consts["ones_f32"], ones_bf16=consts["ones_bf16"],
                 invd=consts["invd"], epsc=consts["epsc"], cm15=consts["cm15"])
        in_maps.append(m)
    return in_maps, perms


def kernel(**inputs):
    in_maps, _ = _prep_inputs(**inputs)
    nc = build_program(dbg=False, upto="full")
    res = run_bass_kernel_spmd(nc, in_maps, core_ids=list(range(8)))
    out = np.empty((2, 4096, 4096), np.float32)
    for c in range(8):
        b, q = c // 4, c % 4
        a = np.asarray(res.results[c]["outT"])
        out[b, q * 1024:(q + 1) * 1024, :] = a.transpose(0, 3, 2, 1).reshape(1024, 4096)
    return out
